# version 26
# baseline (speedup 1.0000x reference)
import contextlib
import os
import numpy as np
import ml_dtypes
import concourse.bass as bass
import concourse.mybir as mybir
from concourse.bass_utils import run_bass_kernel_spmd

F32 = mybir.dt.float32
BF16 = mybir.dt.bfloat16
I32 = mybir.dt.int32
AF = mybir.ActivationFunctionType
ALU = mybir.AluOpType
AX = mybir.AxisListType

NCORES = 8
BPC = 2
S = 2048
D = 2048
HC = 16
DH = 64
MLP = 16384
TB = 512
NTB = S // TB
QHS = 1024
INV_D = 1.0 / D
EPS = 1e-5
GC0 = 0.7978845608028654
GC1 = 0.044715


def _swapmask():
    m = []
    for q in range(0, 32, 2):
        m += [q + 1, q]
    return m


class Sem:
    __slots__ = ("h", "n")

    def __init__(self, h):
        self.h = h
        self.n = 0


class KB:
    def __init__(self, nc, st):
        self.nc = nc
        self.st = st
        self.ops = {k: [] for k in "yvstg"}
        self._semid = 0
        self.qsems = {"y": [], "g": [], "s": []}

    def sem(self):
        h = self.st.enter_context(self.nc.semaphore(f"sm{self._semid}"))
        self._semid += 1
        return Sem(h)

    def sb(self, name, shape, dtype):
        return self.st.enter_context(self.nc.sbuf_tensor(name, shape, dtype))

    def psum(self, name, shape, dtype=F32):
        return self.st.enter_context(self.nc.psum_tensor(name, shape, dtype))

    def wait(self, e, sem, n=None):
        v = sem.n if n is None else n
        if v > 0:
            self.ops[e].append(lambda eng, h=sem.h, v=v: eng.wait_ge(h, v))

    def dma(self, q, out, in_, sem, transpose=False, accum=None):
        sem.n += 16
        if q in self.qsems and sem not in self.qsems[q]:
            self.qsems[q].append(sem)
        kw = {}
        if transpose:
            kw["transpose"] = True
        if accum is not None:
            kw["accum_op"] = accum
        self.ops[q].append(
            lambda eng, o=out, i=in_, h=sem.h, kw=kw: eng.dma_start(o, i, **kw).then_inc(h, 16))

    def op(self, e, fn, incs=()):
        for s, k in incs:
            s.n += k
        incs_h = tuple((s.h, k) for s, k in incs)

        def run(eng, fn=fn, incs_h=incs_h):
            inst = fn(eng)
            for h, k in incs_h:
                inst = inst.then_inc(h, k)
        self.ops[e].append(run)

    def drain(self, e):
        self.ops[e].append(lambda eng: eng.drain())

    def barrier(self, engines="yvstg"):
        for q in ("y", "g", "s"):
            if q in engines:
                for s in self.qsems[q]:
                    self.wait(q, s)
        if not hasattr(self, "_barsem"):
            self._barsem = self.sem()
        bs = self._barsem
        tgt = bs.n + len(engines)
        bs.n = tgt
        for e in engines:
            if e in "vst":
                self.drain(e)
            self.ops[e].append(lambda eng, h=bs.h: eng.sem_inc(h, 1))
        for e in engines:
            self.ops[e].append(lambda eng, h=bs.h, tgt=tgt: eng.wait_ge(h, tgt))




def build_program(dbg=False, stop_after=None):
    nc = bass.Bass("TRN2", target_bir_lowering=False)
    T = {}
    dt = nc.dram_tensor
    T["x"] = dt("x", [BPC, S, D], F32, kind="ExternalInput").ap()
    T["wq"] = dt("wq", [D, D], F32, kind="ExternalInput").ap()
    T["bq"] = dt("bq", [1, D], F32, kind="ExternalInput").ap()
    T["wkv"] = dt("wkv", [1024, D], F32, kind="ExternalInput").ap()
    T["bkv"] = dt("bkv", [1, 1024], F32, kind="ExternalInput").ap()
    T["wo"] = dt("wo", [D, D], F32, kind="ExternalInput").ap()
    T["bo"] = dt("bo", [1, D], F32, kind="ExternalInput").ap()
    T["gw"] = dt("gw", [MLP, D // 2], I32, kind="ExternalInput").ap()
    T["gs"] = dt("gs", [MLP, D // 32], F32, kind="ExternalInput").ap()
    T["uw"] = dt("uw", [MLP, D // 2], I32, kind="ExternalInput").ap()
    T["us"] = dt("us", [MLP, D // 32], F32, kind="ExternalInput").ap()
    T["dw"] = dt("dw", [D, MLP // 2], I32, kind="ExternalInput").ap()
    T["ds"] = dt("ds", [D, MLP // 32], F32, kind="ExternalInput").ap()
    T["ropeC"] = dt("ropeC", [128, S], BF16, kind="ExternalInput").ap()
    T["ropeS"] = dt("ropeS", [128, S], BF16, kind="ExternalInput").ap()
    T["out"] = dt("out", [BPC, S, D], F32, kind="ExternalOutput").ap()
    sk = "ExternalOutput" if dbg else "Internal"
    T["wq_dq"] = dt("wq_dq", [D, D], BF16, kind=sk).ap()
    T["wkv_dq"] = dt("wkv_dq", [1024, D], BF16, kind=sk).ap()
    T["wo_dq"] = dt("wo_dq", [D, D], BF16, kind=sk).ap()
    T["wg_dq"] = dt("wg_dq", [128, HC * 128, 128], BF16, kind=sk).ap()
    T["wu_dq"] = dt("wu_dq", [128, HC * 128, 128], BF16, kind=sk).ap()
    T["wd_dq"] = dt("wd_dq", [2, 128, 1024, 128], BF16, kind=sk).ap()
    T["xn_dq"] = dt("xn_dq", [BPC * S, D], BF16, kind=sk).ap()
    if dbg:
        T["kt_dbg"] = dt("kt_dbg", [128, 8192], BF16, kind="ExternalOutput").ap()
        T["v_dbg"] = dt("v_dbg", [128, 16 * 520], BF16, kind="ExternalOutput").ap()
        T["qt_dbg"] = dt("qt_dbg", [128, 8192], BF16, kind="ExternalOutput").ap()
        T["at_dbg"] = dt("at_dbg", [128, 8192], BF16, kind="ExternalOutput").ap()
        T["atu_dbg"] = dt("atu_dbg", [128, 8192], BF16, kind="ExternalOutput").ap()
        T["rd_dbg"] = dt("rd_dbg", [1, 4096], BF16, kind="ExternalOutput").ap()
        T["slab_dbg"] = dt("slab_dbg", [128, 16384], BF16, kind="ExternalOutput").ap()
        T["xq_dbg"] = dt("xq_dbg", [128, 2048], BF16, kind="ExternalOutput").ap()
        T["xq2_dbg"] = dt("xq2_dbg", [128, 2048], BF16, kind="ExternalOutput").ap()
        T["xq3_dbg"] = dt("xq3_dbg", [128, 2048], BF16, kind="ExternalOutput").ap()
    T["int_dq"] = dt("int_dq", [2, MLP, 1024], BF16, kind=sk).ap()

    st = contextlib.ExitStack()
    with st:
        kb = KB(nc, st)
        slab = kb.sb("slab", [128, 16384], BF16)
        qt = kb.sb("qt", [128, 8192], BF16)
        at = kb.sb("at", [128, 8192], BF16)
        kt = kb.sb("kt", [128, 16384], BF16)
        vsb = kb.sb("vsb", [128, 16 * 520], BF16)
        mega = kb.sb("mega", [128, 32768], BF16)
        ropeC = kb.sb("ropeCs", [128, S], BF16)
        ropeS = kb.sb("ropeSs", [128, S], BF16)
        bqr = kb.sb("bqr", [1, D], BF16)
        bkvr = kb.sb("bkvr", [1, 1024], BF16)
        bor = kb.sb("bor", [1, D], BF16)
        onesr = kb.sb("onesr", [1, D], BF16)
        oneslo = kb.sb("oneslo", [1, 128], BF16)
        oneshi = kb.sb("oneshi", [1, 128], BF16)
        epsr = kb.sb("epsr", [128, 1], F32)
        zrow = kb.sb("zrow", [1, 1], BF16)
        one1r = kb.sb("one1r", [128, 1], F32)
        neg8r = kb.sb("neg8r", [128, 1], F32)
        PS_A = kb.psum("PS_A", [128, 2048], F32)
        PS_B = kb.psum("PS_B", [128, 1024], F32)
        PS_C = kb.psum("PS_C", [128, 1024], F32)

        def mg(c0, cols, dtype=BF16):
            v = mega[:, c0:c0 + cols]
            if dtype is not BF16:
                v = v.bitcast(dtype)
            return v

        rden = mega[0:1, 22528:26624]
        rdenf = mega[0:1, 26624:28672].bitcast(F32)
        lnb = mega[0:1, 28672:30720].bitcast(F32)

        sTb = kb.sem()
        sLf = [kb.sem() for _ in range(2)]
        sCv = [kb.sem() for _ in range(2)]
        sSv = [kb.sem() for _ in range(2)]
        sLp = [kb.sem() for _ in range(2)]
        sFp = [kb.sem() for _ in range(2)]
        sDq = [kb.sem() for _ in range(2)]
        sVint = kb.sem()
        sAct = kb.sem()
        sLut = kb.sem()
        sSq = [kb.sem() for _ in range(2)]
        sSlab = kb.sem()
        sSlabF = kb.sem()
        sLxq = [kb.sem() for _ in range(2)]
        sFxq = [kb.sem() for _ in range(2)]
        sLxL = [kb.sem() for _ in range(2)]
        sSqA = kb.sem()
        sVs1 = kb.sem()
        sSqr = kb.sem()
        sXnV = [kb.sem() for _ in range(2)]
        sXnS = [kb.sem() for _ in range(2)]
        sMMk = kb.sem()
        sEvK = kb.sem()
        sRpK = kb.sem()
        sMMv = kb.sem()
        sEvV = kb.sem()
        sMMq = kb.sem()
        sEvQ = kb.sem()
        sRpQ = kb.sem()
        sSc = kb.sem()
        sScv = kb.sem()
        sExp = kb.sem()
        sPV = kb.sem()
        sEvA = kb.sem()
        sRcp = kb.sem()
        sDen = kb.sem()
        sRmm = kb.sem()
        sNrm = kb.sem()
        sMMo = kb.sem()
        sEvO = kb.sem()
        sSo = [kb.sem() for _ in range(2)]
        sLxo = [kb.sem() for _ in range(2)]
        sXnt = kb.sem()
        sLwg = [kb.sem() for _ in range(3)]
        sFwg = [kb.sem() for _ in range(3)]
        sLwu = [kb.sem() for _ in range(3)]
        sFwu = [kb.sem() for _ in range(3)]
        sGmm = kb.sem()
        sUmm = kb.sem()
        sGev = kb.sem()
        sUev = kb.sem()
        sGfree = kb.sem()
        sGel = kb.sem()
        sTanh = kb.sem()
        sSi = kb.sem()
        sLint = [kb.sem() for _ in range(2)]
        sFint = [kb.sem() for _ in range(2)]
        sLwd = [kb.sem() for _ in range(2)]
        sLint2 = [kb.sem() for _ in range(2)]
        sFint2 = [kb.sem() for _ in range(2)]
        sLwd2 = [kb.sem() for _ in range(2)]
        sSts = [kb.sem() for _ in range(4)]
        sDmm = kb.sem()
        sDev = kb.sem()

        kb.op("v", lambda v: v.memset(onesr[:, :], 1.0))
        kb.op("v", lambda v: v.memset(oneslo[:, :], 0.0))
        kb.op("v", lambda v: v.memset(oneshi[:, :], 0.0))
        kb.op("v", lambda v: v.memset(oneslo[0:1, 0:64], 1.0))
        kb.op("v", lambda v: v.memset(oneshi[0:1, 64:128], 1.0))
        kb.op("v", lambda v: v.memset(epsr[:, :], EPS))
        kb.op("v", lambda v: v.memset(zrow[:, :], 0.0))
        kb.op("v", lambda v: v.memset(one1r[:, :], 1.0))
        kb.op("v", lambda v: v.memset(kt[:, :], 0.0))
        kb.op("v", lambda v: v.memset(neg8r[:, :], -8.0))
        vones = vsb[:, :].rearrange("p (t k c) -> p t k c", t=16, k=8)[:, :, :, 64:65]
        kb.op("v", lambda v, o=vones: v.memset(o, 1.0))

        biasbuf = mg(12288, 10240, F32)
        kb.dma("y", ropeC[:, :], T["ropeC"], sTb)
        kb.dma("y", ropeS[:, :], T["ropeS"], sTb)
        kb.dma("y", biasbuf[0:1, 0:2048], T["bq"], sTb)
        kb.dma("y", biasbuf[0:1, 2048:3072], T["bkv"], sTb)
        kb.dma("y", biasbuf[0:1, 3072:5120], T["bo"], sTb)
        kb.wait("v", sTb)
        kb.op("v", lambda v: v.tensor_copy(bqr[:, :], biasbuf[0:1, 0:2048]))
        kb.op("v", lambda v: v.tensor_copy(bkvr[:, :], biasbuf[0:1, 2048:3072]))
        kb.op("v", lambda v: v.tensor_copy(bor[:, :], biasbuf[0:1, 3072:5120]))

        fbuf = [mg(i * 4096, 4096, F32) for i in range(2)]
        cbuf = [mg(8192 + i * 2048, 2048) for i in range(2)]
        conv_jobs = (
            [(T["wq"][i * 128:(i + 1) * 128, :], T["wq_dq"][i * 128:(i + 1) * 128, :]) for i in range(16)]
            + [(T["wkv"][i * 128:(i + 1) * 128, :], T["wkv_dq"][i * 128:(i + 1) * 128, :]) for i in range(8)]
            + [(T["wo"][i * 128:(i + 1) * 128, :], T["wo_dq"][i * 128:(i + 1) * 128, :]) for i in range(16)])
        for i, (src, dst) in enumerate(conv_jobs):
            sl = i % 2
            rnd = i // 2
            if rnd > 0:
                kb.wait("y", sCv[sl], rnd)
            kb.dma("y", fbuf[sl], src, sLf[sl])
            kb.wait("v", sLf[sl])
            if rnd > 0:
                kb.wait("v", sSv[sl], 16 * rnd)
            kb.op("v", lambda v, a=cbuf[sl], b=fbuf[sl]: v.tensor_copy(a, b), incs=[(sCv[sl], 1)])
            kb.wait("g", sCv[sl])
            kb.dma("g", dst, cbuf[sl], sSv[sl])
        kb.barrier()
        if stop_after == "p0":
            finish(kb, st, nc)
            return nc

        pbuf = [mg(i * 2048, 2048, I32) for i in range(2)]
        scbuf = [mg(4096 + i * 128, 128, F32) for i in range(2)]
        qout = [mg(4608 + i * 2048, 2048) for i in range(2)]
        m_pl = [[mg(8704 + (p * 2 + s_) * 1024, 1024) for s_ in range(2)] for p in range(2)]
        s_pl = [[mg(12800 + (p * 2 + s_) * 1024, 1024) for s_ in range(2)] for p in range(2)]
        t1b = [[mg(16896 + (p * 2 + s_) * 1024, 1024) for s_ in range(2)] for p in range(2)]
        t2b = [mg(25088 + i * 1024, 1024) for i in range(2)]
        vab = [mg(27136 + i * 1024, 1024) for i in range(2)]
        vvb = [mg(29184 + i * 1024, 1024) for i in range(2)]
        sccb = mg(31232, 128, F32)
        deq_jobs = []
        for i in range(128):
            dst = T["wg_dq"][i].rearrange("(t nn) h -> t nn h", t=HC).transpose([1, 0, 2])
            deq_jobs.append((T["gw"][i * 128:(i + 1) * 128, :], T["gs"][i * 128:(i + 1) * 128, :], dst))
        for i in range(128):
            dst = T["wu_dq"][i].rearrange("(t nn) h -> t nn h", t=HC).transpose([1, 0, 2])
            deq_jobs.append((T["uw"][i * 128:(i + 1) * 128, :], T["us"][i * 128:(i + 1) * 128, :], dst))
        for rc in range(16):
            for cb in range(8):
                dst = T["wd_dq"][rc // 8, cb * 16:(cb + 1) * 16,
                                 (rc % 8) * 128:(rc % 8 + 1) * 128, :].transpose([1, 0, 2])
                deq_jobs.append((T["dw"][rc * 128:(rc + 1) * 128, cb * 1024:(cb + 1) * 1024],
                                 T["ds"][rc * 128:(rc + 1) * 128, cb * 64:(cb + 1) * 64], dst))
        miA = [[at[:, (p * 2 + s_) * 2048:(p * 2 + s_) * 2048 + 2048].bitcast(I32)
                for s_ in range(2)] for p in range(2)]
        siA = [[qt[:, (p * 2 + s_) * 2048:(p * 2 + s_) * 2048 + 2048].bitcast(I32)
                for s_ in range(2)] for p in range(2)]
        sccb2 = [mg(31232 + i * 128, 128, F32) for i in range(2)]

        def deq_ints(i):
            sl = i % 2
            rnd = i // 2
            if rnd > 0:
                kb.wait("y", sFp[sl], rnd)
            psrc, ssrc, dst = deq_jobs[i]
            kb.dma("y", pbuf[sl], psrc, sLp[sl])
            kb.dma("y", scbuf[sl], ssrc, sLp[sl])
            kb.wait("v", sLp[sl])
            if i >= 2:
                kb.wait("v", sAct, i - 1)
            w = pbuf[sl]
            kb.op("v", lambda v, w=w, d=miA[0][sl]: v.tensor_single_scalar(d, w, 7, ALU.bitwise_and))
            kb.op("v", lambda v, w=w, d=siA[0][sl]: v.tensor_single_scalar(d, w, 8, ALU.bitwise_and))
            kb.op("v", lambda v, w=w, d=miA[1][sl]: v.tensor_scalar(
                d, w, 4, 7, ALU.logical_shift_right, ALU.bitwise_and))
            kb.op("v", lambda v, w=w, d=siA[1][sl]: v.tensor_scalar(
                d, w, 4, 8, ALU.logical_shift_right, ALU.bitwise_and), incs=[(sVint, 1)])
            kb.op("v", lambda v, a=sccb2[sl], bsc=scbuf[sl]: v.tensor_copy(a, bsc),
                  incs=[(sFp[sl], 1)])

        def deq_act(i):
            sl = i % 2
            kb.wait("s", sVint, i + 1)
            if i >= 2:
                kb.wait("s", sLut, i - 1)
            kb.op("s", lambda s, d=m_pl[0][sl], a=miA[0][sl]: s.copy(d, a))
            kb.op("s", lambda s, d=m_pl[1][sl], a=miA[1][sl]: s.copy(d, a))
            kb.op("s", lambda s, d=s_pl[0][sl], a=siA[0][sl]: s.activation(
                d, a, AF.Identity, scale=-0.25, bias=one1r[:, :]))
            kb.op("s", lambda s, d=s_pl[1][sl], a=siA[1][sl]: s.activation(
                d, a, AF.Identity, scale=-0.25, bias=one1r[:, :]))
            kb.op("s", lambda s, d=t1b[0][sl], a=m_pl[0][sl]: s.activation(
                d, a, AF.Identity, scale=2.0, bias=neg8r[:, :]))
            kb.op("s", lambda s, d=t1b[1][sl], a=m_pl[1][sl]: s.activation(
                d, a, AF.Identity, scale=2.0, bias=neg8r[:, :]), incs=[(sAct, 1)])

        def deq_lut(j):
            sl = j % 2
            rnd = j // 2
            kb.wait("v", sAct, j + 1)
            for p in range(2):
                kb.op("v", lambda v, p=p, sl=sl: v.scalar_tensor_tensor(
                    t2b[p][:, :], m_pl[p][sl][:, :], -2.0, t1b[p][sl][:, :], ALU.add, ALU.max))
            for p in range(2):
                kb.op("v", lambda v, p=p, sl=sl: v.scalar_tensor_tensor(
                    vab[p][:, :], m_pl[p][sl][:, :], 0.5, t2b[p][:, :], ALU.mult, ALU.max))
            for p in range(2):
                incs = [(sLut, 1)] if p == 1 else []
                kb.op("v", lambda v, p=p, sl=sl: v.tensor_tensor(
                    vvb[p][:, :], vab[p][:, :], s_pl[p][sl][:, :], ALU.mult), incs=incs)
            if rnd > 0:
                kb.wait("v", sSq[sl], 16 * rnd)
            scl_b = sccb2[sl][:, :].unsqueeze(2).broadcast_to([128, 64, 16])
            for p in range(2):
                incs = [(sDq[sl], 1)] if p == 1 else []
                kb.op("v", lambda v, p=p, sb_=scl_b, o=qout[sl]: v.tensor_tensor(
                    o[:, p::2].rearrange("q (bb j) -> q bb j", bb=64),
                    vvb[p][:, :].rearrange("q (bb j) -> q bb j", bb=64),
                    sb_, ALU.mult), incs=incs)
            kb.wait("g", sDq[sl], rnd + 1)
            dst = deq_jobs[j][2]
            qsrc = qout[sl].rearrange("p (t h) -> p t h", t=16)
            kb.dma("g", dst[:, 0:8, :], qsrc[:, 0:8, :], sSq[sl])
            kb.dma("g", dst[:, 8:16, :], qsrc[:, 8:16, :], sSq[sl])

        for i in range(len(deq_jobs)):
            deq_ints(i)
            deq_act(i)
            if i >= 1:
                deq_lut(i - 1)
        deq_lut(len(deq_jobs) - 1)
        kb.barrier()
        if stop_after == "p1":
            finish(kb, st, nc)
            return nc

        xqb = [mg(i * 2048, 2048) for i in range(2)]
        ktmp = [mg(4096 + i * 2048, 2048) for i in range(2)]
        ksw = mg(8192, 2048)
        rr1 = mg(10240, 2048)
        rr2 = mg(12288, 2048)
        ptbg = [mg(i * 2048, 2048) for i in range(3)]
        scfp = [mg(6144 + i * 2048, 2048, F32) for i in range(4)]
        scfg = [mg(6144 + i * 4096, 4096, F32) for i in range(2)]
        xbo = [mg(16384 + i * 4096, 4096, F32) for i in range(2)]
        obuf = [mg(24576 + i * 4096, 4096, F32) for i in range(2)]
        xbufL = [mg(i * 4096, 4096, F32) for i in range(2)]
        sqbL = [mg(8192 + i * 4096, 4096, F32) for i in range(2)]
        xnoL = [mg(16384 + i * 2048, 2048) for i in range(2)]
        stL = [mg(20480 + i * 64, 64, F32) for i in range(2)]
        wgb = [mg(i * 512, 512) for i in range(3)]
        wub = [mg(1536 + i * 512, 512) for i in range(3)]
        gsb = [mg(3072 + i * 512, 512) for i in range(4)]
        usb = [mg(5120 + i * 512, 512) for i in range(4)]
        tA = [mg(7168 + i * 512, 512) for i in range(4)]
        tBv = [mg(9216 + i * 512, 512) for i in range(4)]
        tC = [mg(11264 + i * 512, 512) for i in range(4)]
        intb = [mg(13312 + i * 512, 512) for i in range(4)]
        xnt = mg(15360, 8192)
        ilb = [mg(23552 + i * 512, 512) for i in range(2)]
        wdb = [mg(24576 + i * 2048, 2048) for i in range(2)]
        odb = [mg(28672 + i * 2048, 2048, F32) for i in range(2)]

        def zmm(ps_elem, sems):
            kb.op("t", lambda t, p=ps_elem: t.matmul(p, zrow[0:1, 0:1], zrow[0:1, 0:1],
                                                     start=False, stop=True), incs=sems)

        slab_uses = [0]

        def slab_refill(jobs):
            kb.wait("y", sSlabF, slab_uses[0])
            for d, s_ in jobs:
                kb.dma("y", d, s_, sSlab, transpose=True)
            kb.wait("t", sSlab)
            slab_uses[0] += 1

        xq_state = {"ptr": 0, "rnds": [0, 0]}

        def xq_load(src, width):
            i = xq_state["ptr"] % 2
            if xq_state["rnds"][i] > 0:
                kb.wait("y", sFxq[i], xq_state["rnds"][i])
            kb.dma("y", xqb[i][:, 0:width], src, sLxq[i], transpose=True)
            xq_state["rnds"][i] += 1
            xq_state["ptr"] += 1
            kb.wait("t", sLxq[i])
            return i

        for b in range(BPC):
            bSqA, bVs1, bSqr = sSqA.n, sVs1.n, sSqr.n
            bXnV = [s.n for s in sXnV]
            bXnS = [s.n for s in sXnS]
            bLxL = [s.n for s in sLxL]
            for k in range(8):
                for j in range(2):
                    if k > 0:
                        kb.wait("y", sXnV[j], bXnV[j] + k)
                    c = 2 * k + j
                    kb.dma("y", xbufL[j], T["x"][b, c * 128:(c + 1) * 128, :], sLxL[j])
                for j in range(2):
                    kb.wait("v", sLxL[j])
                    kb.op("v", lambda v, j=j: v.reduce_sum(stL[j][:, 0:1], xbufL[j], axis=AX.X))
                for j in range(2):
                    kb.wait("s", sLxL[j], bLxL[j] + 16 * (k + 1))
                    if k > 0:
                        kb.wait("s", sVs1, bVs1 + 2 * k)
                    kb.op("s", lambda s, j=j: s.activation(sqbL[j], xbufL[j], AF.Square,
                                                           accum_out=stL[j][:, 1:2]), incs=[(sSqA, 1)])
                kb.wait("v", sSqA, bSqA + 2 * (k + 1))
                if k > 0:
                    kb.wait("v", sSqr, bSqr + 2 * k)
                for j in range(2):
                    kb.op("v", lambda v, j=j: v.tensor_single_scalar(stL[j][:, 2:3], stL[j][:, 0:1], INV_D, ALU.mult))
                for j in range(2):
                    kb.op("v", lambda v, j=j: v.tensor_mul(stL[j][:, 3:4], stL[j][:, 2:3], stL[j][:, 2:3]))
                for j in range(2):
                    kb.op("v", lambda v, j=j: v.tensor_single_scalar(stL[j][:, 4:5], stL[j][:, 1:2], INV_D, ALU.mult))
                for j in range(2):
                    kb.op("v", lambda v, j=j: v.tensor_sub(stL[j][:, 5:6], stL[j][:, 4:5], stL[j][:, 3:4]),
                          incs=[(sVs1, 1)])
                kb.wait("s", sVs1, bVs1 + 2 * (k + 1))
                for j in range(2):
                    kb.op("s", lambda s, j=j: s.activation(stL[j][:, 6:7], stL[j][:, 5:6], AF.Sqrt,
                                                           bias=epsr[:, :]), incs=[(sSqr, 1)])
                kb.wait("v", sSqr, bSqr + 2 * (k + 1))
                for j in range(2):
                    kb.op("v", lambda v, j=j: v.reciprocal(stL[j][:, 7:8], stL[j][:, 6:7]))
                kb.drain("v")
                for j in range(2):
                    kb.op("v", lambda v, j=j: v.scalar_tensor_tensor(
                        stL[j][:, 8:9], stL[j][:, 2:3], -1.0, stL[j][:, 7:8], ALU.mult, ALU.mult))
                kb.drain("v")
                for j in range(2):
                    if k > 0:
                        kb.wait("v", sXnS[j], bXnS[j] + 16 * k)
                    kb.op("v", lambda v, j=j: v.tensor_scalar(
                        xnoL[j], xbufL[j], stL[j][:, 7:8], stL[j][:, 8:9], ALU.mult, ALU.add),
                        incs=[(sXnV[j], 1)])
                for j in range(2):
                    c = 2 * k + j
                    kb.wait("g", sXnV[j], bXnV[j] + k + 1)
                    kb.dma("g", T["xn_dq"][b * S + c * 128: b * S + (c + 1) * 128, :], xnoL[j], sXnS[j])
            kb.barrier()
            if stop_after == "ln":
                finish(kb, st, nc)
                return nc

            slab_refill([(slab[:, hc * 1024:(hc + 1) * 1024],
                          T["wkv_dq"][0:1024, hc * 128:(hc + 1) * 128]) for hc in range(HC)])
            if dbg and b == 0:
                kb.barrier()
                sDbgS = kb.sem()
                kb.dma("g", T["slab_dbg"], slab[:, :], sDbgS)
                kb.barrier()
            bMMk, bEvK, bRpK = sMMk.n, sEvK.n, sRpK.n
            for pr in range(4):
                if pr > 0:
                    kb.wait("t", sEvK, bEvK + pr)
                for hc in range(HC):
                    i = xq_load(T["xn_dq"][b * S:(b + 1) * S, hc * 128:(hc + 1) * 128], 2048)
                    if dbg and b == 0 and pr == 0 and hc == 0:
                        sDbgX = kb.sem()
                        kb.wait("g", sLxq[i])
                        kb.dma("g", T["xq_dbg"], xqb[i][:, :], sDbgX)
                        kb.wait("t", sDbgX)
                    for q in range(4):
                        incs = [(sFxq[i], 1)] if q == 3 else []
                        kb.op("t", lambda t, i=i, q=q, hc=hc, pr=pr: t.matmul(
                            PS_A[:, q * 512:(q + 1) * 512],
                            slab[:, hc * 1024 + pr * 128: hc * 1024 + (pr + 1) * 128],
                            xqb[i][:, q * 512:(q + 1) * 512],
                            start=(hc == 0), stop=False), incs=incs)
                for q in range(4):
                    incs = [(sMMk, 1)] if q == 3 else []
                    kb.op("t", lambda t, q=q, pr=pr: t.matmul(
                        PS_A[:, q * 512:(q + 1) * 512],
                        bkvr[0:1, pr * 128:(pr + 1) * 128],
                        onesr[0:1, q * 512:(q + 1) * 512],
                        start=False, stop=True), incs=incs)
                kb.wait("v", sMMk, bMMk + pr + 1)
                if pr > 1:
                    kb.wait("v", sRpK, bRpK + pr - 1)
                km = ktmp[pr % 2]
                kb.op("v", lambda v, km=km: v.tensor_copy(km, PS_A[:, :]), incs=[(sEvK, 1)])
                kb.drain("v")
                kb.op("v", lambda v, km=km: v.tensor_mul(rr1, km, ropeC[:, :]))
                kb.op("v", lambda v, km=km: v.stream_shuffle(ksw, km, _swapmask()))
                kb.drain("v")
                kb.op("v", lambda v: v.tensor_mul(rr2, ksw, ropeS[:, :]))
                kb.drain("v")
                kb.op("v", lambda v, pr=pr: v.tensor_add(
                    kt[0:64, (2 * pr) * 2048:(2 * pr + 1) * 2048], rr1[0:64, :], rr2[0:64, :]))
                kb.op("v", lambda v, pr=pr: v.tensor_add(
                    kt[64:128, (2 * pr + 1) * 2048:(2 * pr + 2) * 2048], rr1[64:128, :], rr2[64:128, :]),
                      incs=[(sRpK, 1)])
            bMMv, bEvV = sMMv.n, sEvV.n
            VPS = [PS_B[:, 0:512], PS_B[:, 512:1024], PS_C[:, 0:512], PS_C[:, 512:1024]]
            for tg in range(4):
                if tg > 0:
                    kb.wait("t", sEvV, bEvV + 4 * tg)
                for hc in range(HC):
                    i = xq_load(T["xn_dq"][b * S:(b + 1) * S, hc * 128:(hc + 1) * 128], 2048)
                    for q in range(4):
                        tc = tg * 4 + q
                        incs = [(sFxq[i], 1)] if q == 3 else []
                        kb.op("t", lambda t, i=i, q=q, tc=tc, hc=hc: t.matmul(
                            VPS[q], xqb[i][:, tc * 128:(tc + 1) * 128],
                            slab[:, hc * 1024 + 512: hc * 1024 + 1024],
                            start=(hc == 0), stop=False), incs=incs)
                        if tg == 3 and hc == 15 and q == 3:
                            zmm(PS_C[0:1, 512:513], [(sSlabF, 1)])
                for q in range(4):
                    incs = [(sMMv, 1)] if q == 3 else []
                    kb.op("t", lambda t, q=q: t.matmul(
                        VPS[q], onesr[0:1, 0:128], bkvr[0:1, 512:1024],
                        start=False, stop=True), incs=incs)
                kb.wait("v", sMMv, bMMv + tg + 1)
                for q in range(4):
                    tc = tg * 4 + q
                    dst = vsb[:, tc * 520:(tc + 1) * 520].rearrange("p (k c) -> p k c", k=8)[:, :, 0:64]
                    vsrc = VPS[q].rearrange("p (k c) -> p k c", k=8)
                    incs = [(sEvV, 4)] if q == 3 else []
                    kb.op("v", lambda v, dst=dst, vsrc=vsrc: v.tensor_copy(dst, vsrc), incs=incs)
            kb.barrier()
            if dbg and b == 0:
                sDbg = kb.sem()
                kb.dma("g", T["kt_dbg"], kt[:, 0:8192], sDbg)
                kb.dma("g", T["v_dbg"], vsb[:, :], sDbg)
                kb.barrier()
            if stop_after == "kv":
                finish(kb, st, nc)
                return nc

            for g in range(2):
                for qh in range(2):
                    qrow = b * S + qh * QHS
                    slab_refill([(slab[:, hc * 1024:(hc + 1) * 1024],
                                  T["wq_dq"][g * 1024:(g + 1) * 1024, hc * 128:(hc + 1) * 128])
                                 for hc in range(HC)])
                    QPS = [PS_A[:, 0:1024], PS_A[:, 1024:2048], PS_B[:, :], PS_C[:, :]]
                    bMMq, bEvQ, bRpQ = sMMq.n, sEvQ.n, sRpQ.n
                    for sw in range(2):
                        if sw > 0:
                            kb.wait("t", sEvQ, bEvQ + 4)
                        for hc in range(HC):
                            i = xq_load(T["xn_dq"][qrow:qrow + QHS, hc * 128:(hc + 1) * 128], 1024)
                            for p4 in range(4):
                                pg = sw * 4 + p4
                                stat = slab[:, hc * 1024 + pg * 128: hc * 1024 + (pg + 1) * 128]
                                for q in range(2):
                                    incs = []
                                    if p4 == 3 and q == 1:
                                        incs.append((sFxq[i], 1))
                                    kb.op("t", lambda t, i=i, q=q, hc=hc, stat=stat, p4=p4: t.matmul(
                                        QPS[p4][:, q * 512:(q + 1) * 512],
                                        stat,
                                        xqb[i][:, q * 512:(q + 1) * 512],
                                        start=(hc == 0), stop=False), incs=incs)
                                    if sw == 1 and hc == 15 and p4 == 3 and q == 1:
                                        zmm(PS_C[0:1, 0:1], [(sSlabF, 1)])
                        for p4 in range(4):
                            pg = sw * 4 + p4
                            bstat = bqr[0:1, g * 1024 + pg * 128: g * 1024 + (pg + 1) * 128]
                            for q in range(2):
                                incs = [(sMMq, 1)] if q == 1 else []
                                kb.op("t", lambda t, q=q, bstat=bstat, p4=p4: t.matmul(
                                    QPS[p4][:, q * 512:(q + 1) * 512],
                                    bstat,
                                    onesr[0:1, q * 512:(q + 1) * 512],
                                    start=False, stop=True), incs=incs)
                        for p4 in range(4):
                            pg = sw * 4 + p4
                            kb.wait("v", sMMq, bMMq + sw * 4 + p4 + 1)
                            if sw * 4 + p4 > 1:
                                kb.wait("v", sRpQ, bRpQ + sw * 4 + p4 - 1)
                            km = ktmp[p4 % 2]
                            HQ = (slice(0, 512), slice(512, 1024))
                            for h_ in range(2):
                                incs = [(sEvQ, 1)] if h_ == 1 else []
                                kb.op("v", lambda v, km=km, QPS=QPS, p4=p4, h_=HQ[h_]: v.tensor_copy(
                                    km[:, h_], QPS[p4][:, h_]), incs=incs)
                            cS = ropeC[:, qh * 1024:(qh + 1) * 1024]
                            sS = ropeS[:, qh * 1024:(qh + 1) * 1024]
                            for h_ in range(2):
                                kb.op("v", lambda v, km=km, cS=cS, h_=HQ[h_]: v.tensor_mul(
                                    rr1[:, h_], km[:, h_], cS[:, h_]))
                            for h_ in range(2):
                                kb.op("v", lambda v, km=km, h_=HQ[h_]: v.stream_shuffle(
                                    ksw[:, h_], km[:, h_], _swapmask()))
                            for h_ in range(2):
                                kb.op("v", lambda v, sS=sS, h_=HQ[h_]: v.tensor_mul(
                                    rr2[:, h_], ksw[:, h_], sS[:, h_]))
                            for h_ in range(2):
                                incs = [(sRpQ, 1)] if h_ == 1 else []
                                kb.op("v", lambda v, pg=pg, h_=h_: v.tensor_add(
                                    qt[:, pg * 1024 + h_ * 512: pg * 1024 + (h_ + 1) * 512],
                                    rr1[:, h_ * 512:(h_ + 1) * 512],
                                    rr2[:, h_ * 512:(h_ + 1) * 512]),
                                    incs=incs)
                    kb.wait("t", sRpQ, bRpQ + 8)
                    if dbg and b == 0 and g == 0 and qh == 0:
                        kb.barrier()
                        sDbg2 = kb.sem()
                        kb.dma("g", T["qt_dbg"], qt[:, :], sDbg2)
                        kb.barrier()
                    bEvA, bRcp, bRmm, bNrm = sEvA.n, sRcp.n, sRmm.n, sNrm.n
                    bDen = sDen.n
                    bSc, bScv, bExp, bPV = sSc.n, sScv.n, sExp.n, sPV.n
                    HEAD_ORDER = [0, 4, 1, 5, 2, 6, 3, 7, 8, 12, 9, 13, 10, 14, 11, 15]
                    hi_of = {hl_: i_ for i_, hl_ in enumerate(HEAD_ORDER)}
                    norm_at = {}
                    for j_ in range(8):
                        norm_at[max(hi_of[2 * j_], hi_of[2 * j_ + 1])] = j_
                    norm_pos = sorted(norm_at)
                    nidx = {p_: i_ for i_, p_ in enumerate(norm_pos)}
                    normle = {q_: len([p_ for p_ in norm_pos if p_ <= q_ - 2])
                              for q_ in range(16)}
                    release = {0: None, 1: None}
                    SPS4 = [PS_B[:, 0:512], PS_B[:, 512:1024],
                            PS_C[:, 0:512], PS_C[:, 512:1024]]

                    def hinfo(hi):
                        hl = HEAD_ORDER[hi]
                        kv = (g * 16 + hl) // 4
                        blk = (hl // 8) * 4 + (hl % 4)
                        return (hl, kv, (kv % 2) * 64, (kv // 2) * 2048,
                                ((hl % 8) // 4) * 64, blk * 1024)

                    def emit_score_mm(u, last):
                        hi_ = u // 32
                        kc = (u % 32) // 2
                        q = u % 2
                        _, kv, kprow, kpcol, qprow, qcol = hinfo(hi_)
                        sps = SPS4[u % 4]
                        incs = [(sSc, 1)]
                        kb.op("t", lambda t, sps=sps, q=q, kc=kc, kv=kv, qcol=qcol: t.matmul(
                                  sps,
                                  kt[0:128, kv * 2048 + kc * 128: kv * 2048 + (kc + 1) * 128],
                                  qt[0:128, qcol + q * 512: qcol + (q + 1) * 512],
                                  start=True, stop=True), incs=incs)

                    def emit_pv_mm(m, last):
                        hm = m // 32
                        kcm = (m % 32) // 2
                        qm = m % 2
                        _, kv, _, _, _, _ = hinfo(hm)
                        apsm = PS_A[:, (hm % 2) * 1024:((hm % 2) + 1) * 1024]
                        if kcm == 0 and qm == 0 and release[hm % 2] is not None:
                            rs, rc = release[hm % 2]
                            kb.wait("t", rs, rc)
                        ptbv = mega[:, ((m // 4) % 3) * 2048 + (m % 4) * 512:
                                    ((m // 4) % 3) * 2048 + (m % 4) * 512 + 512]
                        incs = [(sPV, 1)]
                        kb.op("t", lambda t, apsm=apsm, qm=qm, kcm=kcm, kv=kv, ptbv=ptbv: t.matmul(
                            apsm[0:65, qm * 512:(qm + 1) * 512],
                            vsb[:, kcm * 520 + kv * 65: kcm * 520 + (kv + 1) * 65],
                            ptbv,
                            start=(kcm == 0), stop=(kcm == 15)), incs=incs)

                    def emit_rmm(p):
                        j = norm_at[p]
                        sA = hi_of[2 * j] % 4
                        sB = hi_of[2 * j + 1] % 4
                        half = PS_A[:, (p % 2) * 1024:((p % 2) + 1) * 1024]
                        kb.wait("t", sRcp, bRcp + p + 1)
                        kb.wait("t", sEvA, bEvA + p + 1)
                        for qq in range(2):
                            kb.op("t", lambda t, half=half, sA=sA, qq=qq: t.matmul(
                                half[:, qq * 512:(qq + 1) * 512], oneslo[:, :],
                                rden[0:1, sA * 1024 + qq * 512: sA * 1024 + (qq + 1) * 512],
                                start=True, stop=False))
                        for qq in range(2):
                            incs = [(sRmm, 1)] if qq == 1 else []
                            kb.op("t", lambda t, half=half, sB=sB, qq=qq: t.matmul(
                                half[:, qq * 512:(qq + 1) * 512], oneshi[:, :],
                                rden[0:1, sB * 1024 + qq * 512: sB * 1024 + (qq + 1) * 512],
                                start=False, stop=True), incs=incs)

                    def emit_ttnorm(p):
                        j = norm_at[p]
                        half = PS_A[:, (p % 2) * 1024:((p % 2) + 1) * 1024]
                        kb.wait("v", sRmm, bRmm + nidx[p] + 1)
                        kb.op("v", lambda v, j=j, half=half: v.tensor_tensor(
                            at[:, j * 1024:(j + 1) * 1024], at[:, j * 1024:(j + 1) * 1024],
                            half, ALU.mult), incs=[(sNrm, 1)])
                        release[p % 2] = (sNrm, bNrm + nidx[p] + 1)

                    def emit_copy_pair(k):
                        kb.wait("v", sSc, bSc + 2 * k + 2)
                        if k >= 4:
                            kb.wait("v", sExp, bExp + k // 2 - 1)
                        srcp = PS_B if k % 2 == 0 else PS_C
                        kb.op("v", lambda v, k=k, srcp=srcp: v.tensor_copy(scfp[k % 4], srcp[:, :]),
                              incs=[(sScv, 1)])

                    def emit_exp_group(G):
                        kb.wait("s", sScv, bScv + 2 * G + 2)
                        if G >= 3:
                            kb.wait("s", sPV, bPV + 4 * G - 8)
                        kb.op("s", lambda s, G=G: s.activation(
                            ptbg[G % 3], scfg[G % 2], AF.Exp, scale=0.125), incs=[(sExp, 1)])

                    def emit_evac(hm):
                        hl = HEAD_ORDER[hm]
                        apsm = PS_A[:, (hm % 2) * 1024:((hm % 2) + 1) * 1024]
                        kb.wait("v", sPV, bPV + 32 * (hm + 1))
                        kb.op("v", lambda v, apsm=apsm, hl=hl: v.tensor_copy(
                            at[(hl % 2) * 64:(hl % 2) * 64 + 64, (hl // 2) * 1024:(hl // 2 + 1) * 1024],
                            apsm[0:64, :]), incs=[(sEvA, 1)])
                        release[hm % 2] = (sEvA, bEvA + hm + 1)

                    def emit_dencopy(hm):
                        apsm = PS_A[:, (hm % 2) * 1024:((hm % 2) + 1) * 1024]
                        kb.op("v", lambda v, apsm=apsm: v.tensor_copy(
                            rdenf[0:1, 0:1024], apsm[64:65, :]), incs=[(sDen, 1)])

                    def emit_ln(hm):
                        kb.wait("s", sDen, bDen + hm + 1)
                        kb.op("s", lambda s: s.activation(
                            lnb[0:1, 0:1024], rdenf[0:1, 0:1024], AF.Ln))

                    def emit_rexp(hm):
                        if hm >= 4 and normle[hm] > 0:
                            kb.wait("s", sRmm, bRmm + normle[hm])
                        kb.op("s", lambda s, hm=hm: s.activation(
                            rden[0:1, (hm % 4) * 1024:((hm % 4) + 1) * 1024],
                            lnb[0:1, 0:1024], AF.Exp, scale=-1.0), incs=[(sRcp, 1)])

                    def emit_pv_group(H):
                        kb.wait("t", sExp, bExp + H + 1)
                        for j in range(4):
                            emit_pv_mm(4 * H + j, j == 3)

                    for G in range(128):
                        u0 = 4 * G
                        hi, kcq = divmod(u0, 32)
                        if G >= 1:
                            kb.wait("t", sScv, bScv + 2 * G)
                        for j in range(4):
                            emit_score_mm(u0 + j, j == 3)
                        if G >= 3:
                            emit_pv_group(G - 3)
                        if kcq == 20 and hi - 1 in norm_at:
                            emit_rmm(hi - 1)
                        emit_copy_pair(2 * G)
                        emit_copy_pair(2 * G + 1)
                        if u0 >= 32:
                            hm = hi - 1
                            if kcq == 8:
                                emit_evac(hm)
                            elif kcq == 12:
                                emit_dencopy(hm)
                            elif kcq == 24 and hm in norm_at:
                                emit_ttnorm(hm)
                        emit_exp_group(G)
                        if u0 >= 32:
                            if kcq == 16:
                                emit_ln(hi - 1)
                            elif kcq == 20:
                                emit_rexp(hi - 1)
                    emit_pv_group(125)
                    emit_pv_group(126)
                    emit_pv_group(127)
                    emit_evac(15)
                    emit_dencopy(15)
                    emit_ln(15)
                    kb.drain("s")
                    emit_rexp(15)
                    emit_rmm(15)
                    emit_ttnorm(15)
                    if dbg and b == 0 and g == 0 and qh == 0:
                        kb.barrier()
                        sDbg3 = kb.sem()
                        kb.dma("g", T["at_dbg"], at[:, :], sDbg3)
                        kb.dma("g", T["rd_dbg"], rden[:, :], sDbg3)
                        kb.barrier()
                    slab_refill([(slab[:, jj * 2048:(jj + 1) * 2048],
                                  T["wo_dq"][0:2048, (g * 8 + jj) * 128:(g * 8 + jj + 1) * 128])
                                 for jj in range(8)])
                    kb.wait("t", sNrm, bNrm + 8)
                    kb.wait("t", sEvA, bEvA + 16)
                    kb.wait("t", sRcp, bRcp + 16)
                    bMMo, bEvO = sMMo.n, sEvO.n
                    bSo = [s.n for s in sSo]
                    bSo_g = [s.n for s in sSo]
                    for tcl in range(8):
                        OPS = ([PS_B[:, 0:512], PS_B[:, 512:1024], PS_C[:, 0:512], PS_C[:, 512:1024]]
                               if tcl % 2 == 0 else
                               [PS_A[:, 0:512], PS_A[:, 512:1024], PS_A[:, 1024:1536], PS_A[:, 1536:2048]])
                        if tcl >= 2:
                            kb.wait("t", sEvO, bEvO + (tcl - 1) * 4)
                        for jj in range(8):
                            for q in range(4):
                                incs = []
                                if g != 0 and jj == 7 and q == 3:
                                    incs.append((sMMo, 1))
                                kb.op("t", lambda t, OPS=OPS, jj=jj, q=q, tcl=tcl, g=g: t.matmul(
                                    OPS[q], at[:, jj * 1024 + tcl * 128: jj * 1024 + (tcl + 1) * 128],
                                    slab[:, jj * 2048 + q * 512: jj * 2048 + (q + 1) * 512],
                                    start=(jj == 0), stop=(g != 0 and jj == 7)), incs=incs)
                                if tcl == 7 and jj == 7 and q == 3:
                                    zmm(PS_A[0:1, 1536:1537], [(sSlabF, 1)])
                        if g == 0:
                            for q in range(4):
                                incs = [(sMMo, 1)] if q == 3 else []
                                kb.op("t", lambda t, OPS=OPS, q=q: t.matmul(
                                    OPS[q], onesr[0:1, 0:128], bor[0:1, q * 512:(q + 1) * 512],
                                    start=False, stop=True), incs=incs)
                        sl = tcl % 2
                        if g == 0:
                            if tcl == 0:
                                kb.wait("y", sNrm, bNrm + 8)
                            if tcl >= 2:
                                kb.wait("y", sSo[sl], bSo[sl] + 16 * (tcl // 2))
                            kb.dma("y", xbo[sl],
                                   T["x"][b, qh * QHS + tcl * 128: qh * QHS + (tcl + 1) * 128, :], sLxo[sl])
                            kb.wait("v", sLxo[sl])
                            kb.wait("v", sMMo, bMMo + tcl + 1)
                            if tcl >= 2:
                                kb.wait("v", sSo[sl], bSo[sl] + 16 * (tcl // 2))
                            for q in range(4):
                                incs = [(sEvO, 4)] if q == 3 else []
                                kb.op("v", lambda v, OPS=OPS, q=q, sl=sl: v.tensor_add(
                                    obuf[sl][:, q * 512:(q + 1) * 512],
                                    xbo[sl][:, q * 512:(q + 1) * 512], OPS[q]), incs=incs)
                            kb.wait("g", sEvO, bEvO + (tcl + 1) * 4)
                            kb.dma("g", T["out"][b, qh * QHS + tcl * 128: qh * QHS + (tcl + 1) * 128, :],
                                   obuf[sl], sSo[sl])
                        else:
                            kb.wait("v", sMMo, bMMo + tcl + 1)
                            if tcl >= 2:
                                kb.wait("v", sSo[sl], bSo[sl] + 16 * (tcl // 2))
                            for q in range(4):
                                incs = [(sEvO, 4)] if q == 3 else []
                                kb.op("v", lambda v, OPS=OPS, q=q, sl=sl: v.tensor_copy(
                                    obuf[sl][:, q * 512:(q + 1) * 512], OPS[q]), incs=incs)
                            kb.wait("g", sEvO, bEvO + (tcl + 1) * 4)
                            kb.dma("g", T["out"][b, qh * QHS + tcl * 128: qh * QHS + (tcl + 1) * 128, :],
                                   obuf[sl], sSo[sl], accum=ALU.add)
                    kb.barrier()
            if stop_after == "attn":
                finish(kb, st, nc)
                return nc

            wgbig = [mg(i * 2048, 2048) for i in range(2)]
            wubig = [mg(4096 + i * 2048, 2048) for i in range(2)]
            gs2 = [mg(8192 + i * 1024, 1024) for i in range(2)]
            us2 = [mg(10240 + i * 1024, 1024) for i in range(2)]
            tA2 = [mg(12288 + i * 1024, 1024) for i in range(2)]
            tB2 = [mg(14336 + i * 1024, 1024) for i in range(2)]
            tC2 = [mg(16384 + i * 1024, 1024) for i in range(2)]
            ib2 = [mg(18432 + i * 1024, 1024) for i in range(2)]
            il3 = ([mg(20480 + i * 1024, 1024) for i in range(2)]
                   + [mg(27648 + i * 1024, 1024) for i in range(2)])
            wd3 = ([mg(22528 + i * 512, 512) for i in range(2)]
                   + [mg(29696 + i * 512, 512) for i in range(2)])
            od3 = ([mg(23552 + i * 1024, 1024, F32) for i in range(4)]
                   + [mg(30720 + i * 1024, 1024, F32) for i in range(2)])
            xnt2 = slab[:, 0:16384]
            GPS2 = [PS_A[:, 0:1024], PS_B[:, :]]
            UPS2 = [PS_A[:, 1024:2048], PS_C[:, :]]
            DPS8 = [PS_A[:, i * 512:(i + 1) * 512] for i in range(4)] + \
                   [PS_B[:, i * 512:(i + 1) * 512] for i in range(2)] + \
                   [PS_C[:, i * 512:(i + 1) * 512] for i in range(2)]
            TBF = 1024
            for tb in range(2):
                gtb = b * 2 + tb
                par = gtb % 2
                if tb == 0:
                    for hc in range(HC):
                        kb.dma("y", xnt2[:, hc * 1024:(hc + 1) * 1024],
                               T["xn_dq"][b * S + tb * TBF: b * S + (tb + 1) * TBF,
                                          hc * 128:(hc + 1) * 128],
                               sXnt, transpose=True)
                kb.wait("t", sXnt)
                bGmm, bUmm, bGev, bUev, bGfree, bGel, bSi = (
                    sGmm.n, sUmm.n, sGev.n, sUev.n, sGfree.n, sGel.n, sSi.n)
                bTanh = sTanh.n
                nwg = [sFwg[i].n for i in range(2)]
                nwu = [sFwu[i].n for i in range(2)]
                for nq in range(128):
                    sl = nq % 2
                    if nwg[sl] > 0:
                        kb.wait("y", sFwg[sl], nwg[sl])
                    kb.dma("y", wgbig[sl], T["wg_dq"][nq], sLwg[sl], transpose=True)
                    nwg[sl] += 1
                    if nwu[sl] > 0:
                        kb.wait("y", sFwu[sl], nwu[sl])
                    kb.dma("y", wubig[sl], T["wu_dq"][nq], sLwu[sl], transpose=True)
                    nwu[sl] += 1
                    kb.wait("t", sLwg[sl])
                    if nq >= 2:
                        kb.wait("t", sGev, bGev + nq - 1)
                    for hc in range(HC):
                        for q in range(2):
                            incs = [(sFwg[sl], 1)] if (hc == 15 and q == 1) else []
                            kb.op("t", lambda t, sl=sl, hc=hc, q=q: t.matmul(
                                GPS2[sl][:, q * 512:(q + 1) * 512],
                                wgbig[sl][:, hc * 128:(hc + 1) * 128],
                                xnt2[:, hc * 1024 + q * 512: hc * 1024 + (q + 1) * 512],
                                start=(hc == 0), stop=(hc == 15)), incs=incs)
                    zmm(GPS2[sl][0:1, 0:1], [(sGmm, 1)])
                    kb.wait("t", sLwu[sl])
                    if nq >= 2:
                        kb.wait("t", sUev, bUev + nq - 1)
                    for hc in range(HC):
                        for q in range(2):
                            incs = [(sFwu[sl], 1)] if (hc == 15 and q == 1) else []
                            kb.op("t", lambda t, sl=sl, hc=hc, q=q: t.matmul(
                                UPS2[sl][:, q * 512:(q + 1) * 512],
                                wubig[sl][:, hc * 128:(hc + 1) * 128],
                                xnt2[:, hc * 1024 + q * 512: hc * 1024 + (q + 1) * 512],
                                start=(hc == 0), stop=(hc == 15)), incs=incs)
                    zmm(UPS2[sl][0:1, 0:1], [(sUmm, 1)])
                    kb.wait("v", sGmm, bGmm + nq + 1)
                    if nq >= 2:
                        kb.wait("v", sGfree, bGfree + nq - 1)
                    kb.op("v", lambda v, sl=sl: v.tensor_copy(gs2[sl], GPS2[sl]), incs=[(sGev, 1)])
                    kb.wait("v", sUmm, bUmm + nq + 1)
                    kb.op("v", lambda v, sl=sl: v.tensor_copy(us2[sl], UPS2[sl]), incs=[(sUev, 1)])
                    H2 = (slice(0, 512), slice(512, 1024))
                    for h_ in range(2):
                        kb.op("v", lambda v, sl=sl, h_=H2[h_]: v.tensor_mul(
                            tA2[sl][:, h_], gs2[sl][:, h_], gs2[sl][:, h_]))
                    for h_ in range(2):
                        kb.op("v", lambda v, sl=sl, h_=H2[h_]: v.tensor_scalar(
                            tB2[sl][:, h_], tA2[sl][:, h_], GC1 * GC0, GC0, ALU.mult, ALU.add))
                    for h_ in range(2):
                        incs = [(sGel, 1)] if h_ == 1 else []
                        kb.op("v", lambda v, sl=sl, h_=H2[h_]: v.tensor_mul(
                            tC2[sl][:, h_], tB2[sl][:, h_], gs2[sl][:, h_]), incs=incs)
                    kb.wait("s", sGel, bGel + nq + 1)
                    kb.op("s", lambda s, sl=sl: s.activation(tA2[sl], tC2[sl], AF.Tanh),
                          incs=[(sTanh, 1)])
                    kb.wait("v", sTanh, bTanh + nq + 1)
                    for h_ in range(2):
                        kb.op("v", lambda v, sl=sl, h_=H2[h_]: v.tensor_scalar(
                            tB2[sl][:, h_], tA2[sl][:, h_], 0.5, 0.5, ALU.mult, ALU.add))
                    for h_ in range(2):
                        kb.op("v", lambda v, sl=sl, h_=H2[h_]: v.tensor_mul(
                            tC2[sl][:, h_], gs2[sl][:, h_], us2[sl][:, h_]))
                    if nq >= 2:
                        kb.wait("v", sSi, bSi + 16 * (nq - 1))
                    for h_ in range(2):
                        incs = [(sGfree, 1)] if h_ == 1 else []
                        kb.op("v", lambda v, sl=sl, h_=H2[h_]: v.tensor_mul(
                            ib2[sl][:, h_], tB2[sl][:, h_], tC2[sl][:, h_]), incs=incs)
                    kb.wait("g", sGfree, bGfree + nq + 1)
                    kb.dma("g", T["int_dq"][par, nq * 128:(nq + 1) * 128, :], ib2[sl], sSi)
                if stop_after == "ffn1g" and b == 0 and tb == 0:
                    kb.barrier()
                    finish(kb, st, nc)
                    return nc
                kb.wait("y", sSi)
                kb.wait("s", sSi)
                kb.wait("t", sGev)
                kb.wait("t", sUev)
                bDmm, bDev = sDmm.n, sDev.n
                sSts6 = sSts + [sLf[0], sLf[1]]
                bSts = [s.n for s in sSts6]
                sLint4 = sLint + sLint2
                sFint4 = sFint + sFint2
                sLwd4 = sLwd + sLwd2
                nint = [s.n for s in sFint4]
                pending = []

                def mk_store(k, tc, r0, c0):
                    slot = tc % 6
                    qs = "g"

                    def emit():
                        kb.wait(qs, sDev, bDev + k + 1)
                        kb.dma(qs, T["out"][b, r0:r0 + 128, c0:c0 + 512],
                               od3[slot], sSts6[slot], accum=ALU.add)
                    return emit

                for oq in range(4):
                    oh, ohf = oq // 2, oq % 2
                    if oq > 0:
                        kb.wait("t", sDev, bDev + oq * 8)

                    lcnt = {}

                    def emit_load(ncc, oh=oh, ohf=ohf):
                        sl4 = ncc % 4
                        qn = "g" if ncc % 2 == 0 else "s"
                        if nint[sl4] > 0:
                            kb.wait(qn, sFint4[sl4], nint[sl4])
                            kb.wait("y", sFint4[sl4], nint[sl4])
                        kb.dma(qn, il3[sl4],
                               T["int_dq"][par, ncc * 128:(ncc + 1) * 128, 0:1024],
                               sLint4[sl4])
                        kb.dma("y", wd3[sl4],
                               T["wd_dq"][oh, ncc, ohf * 512:(ohf + 1) * 512, :],
                               sLwd4[sl4], transpose=True)
                        nint[sl4] += 1
                        lcnt[ncc] = 16 * nint[sl4]

                    for ncc in range(4):
                        emit_load(ncc)
                    for f in pending:
                        f()
                    pending = []
                    for ncc in range(128):
                        if ncc + 4 < 128:
                            emit_load(ncc + 4)
                        if oq == 3 and ncc % 8 == 4 and tb == 0:
                            hc_n = ncc // 8
                            kb.dma("y", xnt2[:, hc_n * 1024:(hc_n + 1) * 1024],
                                   T["xn_dq"][b * S + TBF: b * S + 2 * TBF,
                                              hc_n * 128:(hc_n + 1) * 128],
                                   sXnt, transpose=True)
                        sl4 = ncc % 4
                        kb.wait("t", sLint4[sl4], lcnt[ncc])
                        kb.wait("t", sLwd4[sl4], lcnt[ncc])
                        for tc in range(8):
                            incs = [(sFint4[sl4], 1)] if tc == 7 else []
                            kb.op("t", lambda t, sl4=sl4, tc=tc, ncc=ncc: t.matmul(
                                DPS8[tc],
                                il3[sl4][:, tc * 128:(tc + 1) * 128],
                                wd3[sl4][:, 0:512],
                                start=(ncc == 0), stop=(ncc == 127)), incs=incs)
                    zmm(DPS8[7][0:1, 0:1], [(sDmm, 1)])
                    for tc in range(8):
                        k = oq * 8 + tc
                        slot = tc % 6
                        kb.wait("v", sDmm, bDmm + oq + 1)
                        need = oq * (2 if slot < 2 else 1) + (1 if tc >= 6 else 0)
                        if need > 0:
                            kb.wait("v", sSts6[slot], bSts[slot] + 16 * need)
                        kb.op("v", lambda v, slot=slot, tc=tc: v.tensor_copy(
                            od3[slot], DPS8[tc]), incs=[(sDev, 1)])
                        pending.append(mk_store(k, tc, tb * TBF + tc * 128, oq * 512))
                for f in pending:
                    f()
                kb.barrier()
                if stop_after == "ffn1" and b == 0 and tb == 0:
                    finish(kb, st, nc)
                    return nc

        finish(kb, st, nc)
    return nc


def finish(kb, st, nc):
    kb.barrier()
    block = st.enter_context(nc.Block())

    @block.sync
    def _(eng):
        for f in kb.ops["y"]:
            f(eng)

    @block.vector
    def _(eng):
        for f in kb.ops["v"]:
            f(eng)

    @block.scalar
    def _(eng):
        for f in kb.ops["s"]:
            f(eng)

    @block.tensor
    def _(eng):
        for f in kb.ops["t"]:
            f(eng)

    @block.gpsimd
    def _(eng):
        for f in kb.ops["g"]:
            f(eng)


Q_HEAD_ORDER = [g * 16 + (pl // 4) * 8 + (pl % 4) + dd
                for g in range(2) for pl in range(8) for dd in (0, 4)]


def permute_q(wq, bq):
    wq_p = wq.reshape(32, 64, D)[Q_HEAD_ORDER].reshape(D, D)
    bq_p = bq.reshape(32, 64)[Q_HEAD_ORDER].reshape(1, D)
    return np.ascontiguousarray(wq_p), np.ascontiguousarray(bq_p)


def make_rope_tables():
    i = np.arange(0, DH, 2, dtype=np.float64) / DH
    freq = 1.0 / (10000.0 ** i)
    t = np.arange(S, dtype=np.float64)
    th = t[None, :] * freq[:, None]
    cos = np.cos(th)
    sin = np.sin(th)
    C64 = np.repeat(cos, 2, axis=0)
    S64 = np.repeat(sin, 2, axis=0)
    sign = np.where((np.arange(DH) % 2) == 0, -1.0, 1.0)[:, None]
    S64 = S64 * sign
    C = np.concatenate([C64, C64], axis=0)
    Sg = np.concatenate([S64, S64], axis=0)
    return (np.ascontiguousarray(C).astype(ml_dtypes.bfloat16),
            np.ascontiguousarray(Sg).astype(ml_dtypes.bfloat16))


def kernel(L0_attn_x, L0_attn_W_q, L0_attn_b_q, L0_attn_W_kv, L0_attn_b_kv,
           L0_attn_W_out, L0_attn_b_out, L0_ffn_gate_W_packed, L0_ffn_gate_scales,
           L0_ffn_up_W_packed, L0_ffn_up_scales, L0_ffn_down_W_packed,
           L0_ffn_down_scales):
    x = np.ascontiguousarray(np.asarray(L0_attn_x, dtype=np.float32))
    wq = np.ascontiguousarray(np.asarray(L0_attn_W_q, dtype=np.float32))
    bq = np.ascontiguousarray(np.asarray(L0_attn_b_q, dtype=np.float32)).reshape(1, D)
    wq, bq = permute_q(wq, bq)
    wkv = np.ascontiguousarray(np.asarray(L0_attn_W_kv, dtype=np.float32))
    bkv = np.ascontiguousarray(np.asarray(L0_attn_b_kv, dtype=np.float32)).reshape(1, 1024)
    wo = np.ascontiguousarray(np.asarray(L0_attn_W_out, dtype=np.float32))
    bo = np.ascontiguousarray(np.asarray(L0_attn_b_out, dtype=np.float32)).reshape(1, D)
    gw = np.ascontiguousarray(np.asarray(L0_ffn_gate_W_packed, dtype=np.int32))
    gs = np.ascontiguousarray(np.asarray(L0_ffn_gate_scales, dtype=np.float32))
    uw = np.ascontiguousarray(np.asarray(L0_ffn_up_W_packed, dtype=np.int32))
    us = np.ascontiguousarray(np.asarray(L0_ffn_up_scales, dtype=np.float32))
    dw = np.ascontiguousarray(np.asarray(L0_ffn_down_W_packed, dtype=np.int32))
    ds = np.ascontiguousarray(np.asarray(L0_ffn_down_scales, dtype=np.float32))
    ropec, ropes = make_rope_tables()

    os.environ.setdefault("GAUGE_NTFF_IGNORE_DMA", "1")

    nc = build_program()
    in_maps = []
    for c in range(NCORES):
        in_maps.append(dict(
            x=x[c * BPC:(c + 1) * BPC], wq=wq, bq=bq, wkv=wkv, bkv=bkv, wo=wo, bo=bo,
            gw=gw, gs=gs, uw=uw, us=us, dw=dw, ds=ds, ropeC=ropec, ropeS=ropes))
    res = run_bass_kernel_spmd(nc, in_maps, core_ids=list(range(NCORES)))
    out = np.concatenate([np.asarray(r["out"], dtype=np.float32) for r in res.results], axis=0)
    return out.reshape(16, S, D)


# version 27
# speedup vs baseline: 1.0491x; 1.0491x over previous
import contextlib
import os
import numpy as np
import ml_dtypes
import concourse.bass as bass
import concourse.mybir as mybir
from concourse.bass_utils import run_bass_kernel_spmd

F32 = mybir.dt.float32
BF16 = mybir.dt.bfloat16
I32 = mybir.dt.int32
AF = mybir.ActivationFunctionType
ALU = mybir.AluOpType
AX = mybir.AxisListType

NCORES = 8
BPC = 2
S = 2048
D = 2048
HC = 16
DH = 64
MLP = 16384
TB = 512
NTB = S // TB
QHS = 1024
INV_D = 1.0 / D
EPS = 1e-5
GC0 = 0.7978845608028654
GC1 = 0.044715


def _swapmask():
    m = []
    for q in range(0, 32, 2):
        m += [q + 1, q]
    return m


class Sem:
    __slots__ = ("h", "n")

    def __init__(self, h):
        self.h = h
        self.n = 0


class KB:
    def __init__(self, nc, st):
        self.nc = nc
        self.st = st
        self.ops = {k: [] for k in "yvstg"}
        self._semid = 0
        self.qsems = {"y": [], "g": [], "s": []}

    def sem(self):
        h = self.st.enter_context(self.nc.semaphore(f"sm{self._semid}"))
        self._semid += 1
        return Sem(h)

    def sb(self, name, shape, dtype):
        return self.st.enter_context(self.nc.sbuf_tensor(name, shape, dtype))

    def psum(self, name, shape, dtype=F32):
        return self.st.enter_context(self.nc.psum_tensor(name, shape, dtype))

    def wait(self, e, sem, n=None):
        v = sem.n if n is None else n
        if v > 0:
            self.ops[e].append(lambda eng, h=sem.h, v=v: eng.wait_ge(h, v))

    def dma(self, q, out, in_, sem, transpose=False, accum=None):
        sem.n += 16
        if q in self.qsems and sem not in self.qsems[q]:
            self.qsems[q].append(sem)
        kw = {}
        if transpose:
            kw["transpose"] = True
        if accum is not None:
            kw["accum_op"] = accum
        self.ops[q].append(
            lambda eng, o=out, i=in_, h=sem.h, kw=kw: eng.dma_start(o, i, **kw).then_inc(h, 16))

    def op(self, e, fn, incs=()):
        for s, k in incs:
            s.n += k
        incs_h = tuple((s.h, k) for s, k in incs)

        def run(eng, fn=fn, incs_h=incs_h):
            inst = fn(eng)
            for h, k in incs_h:
                inst = inst.then_inc(h, k)
        self.ops[e].append(run)

    def drain(self, e):
        self.ops[e].append(lambda eng: eng.drain())

    def barrier(self, engines="yvstg"):
        for q in ("y", "g", "s"):
            if q in engines:
                for s in self.qsems[q]:
                    self.wait(q, s)
        if not hasattr(self, "_barsem"):
            self._barsem = self.sem()
        bs = self._barsem
        tgt = bs.n + len(engines)
        bs.n = tgt
        for e in engines:
            if e in "vst":
                self.drain(e)
            self.ops[e].append(lambda eng, h=bs.h: eng.sem_inc(h, 1))
        for e in engines:
            self.ops[e].append(lambda eng, h=bs.h, tgt=tgt: eng.wait_ge(h, tgt))




def build_program(dbg=False, stop_after=None):
    nc = bass.Bass("TRN2", target_bir_lowering=False)
    T = {}
    dt = nc.dram_tensor
    T["x"] = dt("x", [BPC, S, D], F32, kind="ExternalInput").ap()
    T["wq"] = dt("wq", [D, D], F32, kind="ExternalInput").ap()
    T["bq"] = dt("bq", [1, D], F32, kind="ExternalInput").ap()
    T["wkv"] = dt("wkv", [1024, D], F32, kind="ExternalInput").ap()
    T["bkv"] = dt("bkv", [1, 1024], F32, kind="ExternalInput").ap()
    T["wo"] = dt("wo", [D, D], F32, kind="ExternalInput").ap()
    T["bo"] = dt("bo", [1, D], F32, kind="ExternalInput").ap()
    T["gw"] = dt("gw", [MLP, D // 2], I32, kind="ExternalInput").ap()
    T["gs"] = dt("gs", [MLP, D // 32], F32, kind="ExternalInput").ap()
    T["uw"] = dt("uw", [MLP, D // 2], I32, kind="ExternalInput").ap()
    T["us"] = dt("us", [MLP, D // 32], F32, kind="ExternalInput").ap()
    T["dw"] = dt("dw", [D, MLP // 2], I32, kind="ExternalInput").ap()
    T["ds"] = dt("ds", [D, MLP // 32], F32, kind="ExternalInput").ap()
    T["ropeC"] = dt("ropeC", [128, S], BF16, kind="ExternalInput").ap()
    T["ropeS"] = dt("ropeS", [128, S], BF16, kind="ExternalInput").ap()
    T["out"] = dt("out", [BPC, S, D], F32, kind="ExternalOutput").ap()
    sk = "ExternalOutput" if dbg else "Internal"
    T["wq_dq"] = dt("wq_dq", [D, D], BF16, kind=sk).ap()
    T["wkv_dq"] = dt("wkv_dq", [1024, D], BF16, kind=sk).ap()
    T["wo_dq"] = dt("wo_dq", [D, D], BF16, kind=sk).ap()
    T["wg_dq"] = dt("wg_dq", [128, HC * 128, 128], BF16, kind=sk).ap()
    T["wu_dq"] = dt("wu_dq", [128, HC * 128, 128], BF16, kind=sk).ap()
    T["wd_dq"] = dt("wd_dq", [2, 128, 1024, 128], BF16, kind=sk).ap()
    T["xn_dq"] = dt("xn_dq", [BPC * S, D], BF16, kind=sk).ap()
    if dbg:
        T["kt_dbg"] = dt("kt_dbg", [128, 8192], BF16, kind="ExternalOutput").ap()
        T["v_dbg"] = dt("v_dbg", [128, 16 * 520], BF16, kind="ExternalOutput").ap()
        T["qt_dbg"] = dt("qt_dbg", [128, 8192], BF16, kind="ExternalOutput").ap()
        T["at_dbg"] = dt("at_dbg", [128, 8192], BF16, kind="ExternalOutput").ap()
        T["atu_dbg"] = dt("atu_dbg", [128, 8192], BF16, kind="ExternalOutput").ap()
        T["rd_dbg"] = dt("rd_dbg", [1, 4096], BF16, kind="ExternalOutput").ap()
        T["slab_dbg"] = dt("slab_dbg", [128, 16384], BF16, kind="ExternalOutput").ap()
        T["xq_dbg"] = dt("xq_dbg", [128, 2048], BF16, kind="ExternalOutput").ap()
        T["xq2_dbg"] = dt("xq2_dbg", [128, 2048], BF16, kind="ExternalOutput").ap()
        T["xq3_dbg"] = dt("xq3_dbg", [128, 2048], BF16, kind="ExternalOutput").ap()
    T["int_dq"] = dt("int_dq", [2, MLP, 1024], BF16, kind=sk).ap()

    st = contextlib.ExitStack()
    with st:
        kb = KB(nc, st)
        slab = kb.sb("slab", [128, 16384], BF16)
        qt = kb.sb("qt", [128, 8192], BF16)
        at = kb.sb("at", [128, 8192], BF16)
        kt = kb.sb("kt", [128, 16384], BF16)
        vsb = kb.sb("vsb", [128, 16 * 520], BF16)
        mega = kb.sb("mega", [128, 32768], BF16)
        ropeC = kb.sb("ropeCs", [128, S], BF16)
        ropeS = kb.sb("ropeSs", [128, S], BF16)
        bqr = kb.sb("bqr", [1, D], BF16)
        bkvr = kb.sb("bkvr", [1, 1024], BF16)
        bor = kb.sb("bor", [1, D], BF16)
        onesr = kb.sb("onesr", [1, D], BF16)
        oneslo = kb.sb("oneslo", [1, 128], BF16)
        oneshi = kb.sb("oneshi", [1, 128], BF16)
        epsr = kb.sb("epsr", [128, 1], F32)
        zrow = kb.sb("zrow", [1, 1], BF16)
        one1r = kb.sb("one1r", [128, 1], F32)
        neg8r = kb.sb("neg8r", [128, 1], F32)
        PS_A = kb.psum("PS_A", [128, 2048], F32)
        PS_B = kb.psum("PS_B", [128, 1024], F32)
        PS_C = kb.psum("PS_C", [128, 1024], F32)

        def mg(c0, cols, dtype=BF16):
            v = mega[:, c0:c0 + cols]
            if dtype is not BF16:
                v = v.bitcast(dtype)
            return v

        rden = mega[0:1, 22528:26624]
        rdenf = mega[0:1, 26624:28672].bitcast(F32)
        lnb = mega[0:1, 28672:30720].bitcast(F32)

        sTb = kb.sem()
        sLf = [kb.sem() for _ in range(2)]
        sCv = [kb.sem() for _ in range(2)]
        sSv = [kb.sem() for _ in range(2)]
        sLp = [kb.sem() for _ in range(2)]
        sFp = [kb.sem() for _ in range(2)]
        sDq = [kb.sem() for _ in range(2)]
        sVint = kb.sem()
        sAct = kb.sem()
        sLut = kb.sem()
        sSq = [kb.sem() for _ in range(2)]
        sSlab = kb.sem()
        sSlabF = kb.sem()
        sLxq = [kb.sem() for _ in range(2)]
        sFxq = [kb.sem() for _ in range(2)]
        sLxL = [kb.sem() for _ in range(2)]
        sSqA = kb.sem()
        sVs1 = kb.sem()
        sSqr = kb.sem()
        sXnV = [kb.sem() for _ in range(2)]
        sXnS = [kb.sem() for _ in range(2)]
        sMMk = kb.sem()
        sEvK = kb.sem()
        sRpK = kb.sem()
        sMMv = kb.sem()
        sEvV = kb.sem()
        sMMq = kb.sem()
        sEvQ = kb.sem()
        sRpQ = kb.sem()
        sSc = kb.sem()
        sScv = kb.sem()
        sExp = kb.sem()
        sPV = kb.sem()
        sEvA = kb.sem()
        sRcp = kb.sem()
        sDen = kb.sem()
        sRmm = kb.sem()
        sNrm = kb.sem()
        sMMo = kb.sem()
        sEvO = kb.sem()
        sSo = [kb.sem() for _ in range(2)]
        sLxo = [kb.sem() for _ in range(2)]
        sXnt = kb.sem()
        sLwg = [kb.sem() for _ in range(3)]
        sFwg = [kb.sem() for _ in range(3)]
        sLwu = [kb.sem() for _ in range(3)]
        sFwu = [kb.sem() for _ in range(3)]
        sGmm = kb.sem()
        sUmm = kb.sem()
        sGev = kb.sem()
        sUev = kb.sem()
        sGfree = kb.sem()
        sGel = kb.sem()
        sTanh = kb.sem()
        sSi = kb.sem()
        sLint = [kb.sem() for _ in range(2)]
        sFint = [kb.sem() for _ in range(2)]
        sLwd = [kb.sem() for _ in range(2)]
        sLint2 = [kb.sem() for _ in range(2)]
        sFint2 = [kb.sem() for _ in range(2)]
        sLwd2 = [kb.sem() for _ in range(2)]
        sSts = [kb.sem() for _ in range(4)]
        sDmm = kb.sem()
        sDev = kb.sem()

        kb.op("v", lambda v: v.memset(onesr[:, :], 1.0))
        kb.op("v", lambda v: v.memset(oneslo[:, :], 0.0))
        kb.op("v", lambda v: v.memset(oneshi[:, :], 0.0))
        kb.op("v", lambda v: v.memset(oneslo[0:1, 0:64], 1.0))
        kb.op("v", lambda v: v.memset(oneshi[0:1, 64:128], 1.0))
        kb.op("v", lambda v: v.memset(epsr[:, :], EPS))
        kb.op("v", lambda v: v.memset(zrow[:, :], 0.0))
        kb.op("v", lambda v: v.memset(one1r[:, :], 1.0))
        kb.op("v", lambda v: v.memset(kt[:, :], 0.0))
        kb.op("v", lambda v: v.memset(neg8r[:, :], -8.0))
        vones = vsb[:, :].rearrange("p (t k c) -> p t k c", t=16, k=8)[:, :, :, 64:65]
        kb.op("v", lambda v, o=vones: v.memset(o, 1.0))

        biasbuf = mg(12288, 10240, F32)
        kb.dma("y", ropeC[:, :], T["ropeC"], sTb)
        kb.dma("y", ropeS[:, :], T["ropeS"], sTb)
        kb.dma("y", biasbuf[0:1, 0:2048], T["bq"], sTb)
        kb.dma("y", biasbuf[0:1, 2048:3072], T["bkv"], sTb)
        kb.dma("y", biasbuf[0:1, 3072:5120], T["bo"], sTb)
        kb.wait("v", sTb)
        kb.op("v", lambda v: v.tensor_copy(bqr[:, :], biasbuf[0:1, 0:2048]))
        kb.op("v", lambda v: v.tensor_copy(bkvr[:, :], biasbuf[0:1, 2048:3072]))
        kb.op("v", lambda v: v.tensor_copy(bor[:, :], biasbuf[0:1, 3072:5120]))

        fbuf = [mg(i * 4096, 4096, F32) for i in range(2)]
        cbuf = [mg(8192 + i * 2048, 2048) for i in range(2)]
        conv_jobs = (
            [(T["wq"][i * 128:(i + 1) * 128, :], T["wq_dq"][i * 128:(i + 1) * 128, :]) for i in range(16)]
            + [(T["wkv"][i * 128:(i + 1) * 128, :], T["wkv_dq"][i * 128:(i + 1) * 128, :]) for i in range(8)]
            + [(T["wo"][i * 128:(i + 1) * 128, :], T["wo_dq"][i * 128:(i + 1) * 128, :]) for i in range(16)])
        for i, (src, dst) in enumerate(conv_jobs):
            sl = i % 2
            rnd = i // 2
            if rnd > 0:
                kb.wait("y", sCv[sl], rnd)
            kb.dma("y", fbuf[sl], src, sLf[sl])
            kb.wait("v", sLf[sl])
            if rnd > 0:
                kb.wait("v", sSv[sl], 16 * rnd)
            kb.op("v", lambda v, a=cbuf[sl], b=fbuf[sl]: v.tensor_copy(a, b), incs=[(sCv[sl], 1)])
            kb.wait("g", sCv[sl])
            kb.dma("g", dst, cbuf[sl], sSv[sl])
        kb.barrier()
        if stop_after == "p0":
            finish(kb, st, nc)
            return nc

        pbuf = [mg(i * 2048, 2048, I32) for i in range(2)]
        scbuf = [mg(4096 + i * 128, 128, F32) for i in range(2)]
        qout = [mg(4608 + i * 2048, 2048) for i in range(2)]
        m_pl = [[mg(8704 + (p * 2 + s_) * 1024, 1024) for s_ in range(2)] for p in range(2)]
        s_pl = [[mg(12800 + (p * 2 + s_) * 1024, 1024) for s_ in range(2)] for p in range(2)]
        t1b = [[mg(16896 + (p * 2 + s_) * 1024, 1024) for s_ in range(2)] for p in range(2)]
        t2b = [mg(25088 + i * 1024, 1024) for i in range(2)]
        vab = [mg(27136 + i * 1024, 1024) for i in range(2)]
        vvb = [mg(29184 + i * 1024, 1024) for i in range(2)]
        sccb = mg(31232, 128, F32)
        deq_jobs = []
        for i in range(128):
            dst = T["wg_dq"][i].rearrange("(t nn) h -> t nn h", t=HC).transpose([1, 0, 2])
            deq_jobs.append((T["gw"][i * 128:(i + 1) * 128, :], T["gs"][i * 128:(i + 1) * 128, :], dst))
        for i in range(128):
            dst = T["wu_dq"][i].rearrange("(t nn) h -> t nn h", t=HC).transpose([1, 0, 2])
            deq_jobs.append((T["uw"][i * 128:(i + 1) * 128, :], T["us"][i * 128:(i + 1) * 128, :], dst))
        for rc in range(16):
            for cb in range(8):
                dst = T["wd_dq"][rc // 8, cb * 16:(cb + 1) * 16,
                                 (rc % 8) * 128:(rc % 8 + 1) * 128, :].transpose([1, 0, 2])
                deq_jobs.append((T["dw"][rc * 128:(rc + 1) * 128, cb * 1024:(cb + 1) * 1024],
                                 T["ds"][rc * 128:(rc + 1) * 128, cb * 64:(cb + 1) * 64], dst))
        miA = [[at[:, (p * 2 + s_) * 2048:(p * 2 + s_) * 2048 + 2048].bitcast(I32)
                for s_ in range(2)] for p in range(2)]
        siA = [[qt[:, (p * 2 + s_) * 2048:(p * 2 + s_) * 2048 + 2048].bitcast(I32)
                for s_ in range(2)] for p in range(2)]
        sccb2 = [mg(31232 + i * 128, 128, F32) for i in range(2)]

        def deq_ints(i):
            sl = i % 2
            rnd = i // 2
            if rnd > 0:
                kb.wait("y", sFp[sl], rnd)
            psrc, ssrc, dst = deq_jobs[i]
            kb.dma("y", pbuf[sl], psrc, sLp[sl])
            kb.dma("y", scbuf[sl], ssrc, sLp[sl])
            kb.wait("v", sLp[sl])
            if i >= 2:
                kb.wait("v", sAct, i - 1)
            w = pbuf[sl]
            kb.op("v", lambda v, w=w, d=miA[0][sl]: v.tensor_single_scalar(d, w, 7, ALU.bitwise_and))
            kb.op("v", lambda v, w=w, d=siA[0][sl]: v.tensor_single_scalar(d, w, 8, ALU.bitwise_and))
            kb.op("v", lambda v, w=w, d=miA[1][sl]: v.tensor_scalar(
                d, w, 4, 7, ALU.logical_shift_right, ALU.bitwise_and))
            kb.op("v", lambda v, w=w, d=siA[1][sl]: v.tensor_scalar(
                d, w, 4, 8, ALU.logical_shift_right, ALU.bitwise_and), incs=[(sVint, 1)])
            kb.op("v", lambda v, a=sccb2[sl], bsc=scbuf[sl]: v.tensor_copy(a, bsc),
                  incs=[(sFp[sl], 1)])

        def deq_act(i):
            sl = i % 2
            kb.wait("s", sVint, i + 1)
            if i >= 2:
                kb.wait("s", sLut, i - 1)
            kb.op("s", lambda s, d=m_pl[0][sl], a=miA[0][sl]: s.copy(d, a))
            kb.op("s", lambda s, d=m_pl[1][sl], a=miA[1][sl]: s.copy(d, a))
            kb.op("s", lambda s, d=s_pl[0][sl], a=siA[0][sl]: s.activation(
                d, a, AF.Identity, scale=-0.25, bias=one1r[:, :]))
            kb.op("s", lambda s, d=s_pl[1][sl], a=siA[1][sl]: s.activation(
                d, a, AF.Identity, scale=-0.25, bias=one1r[:, :]))
            kb.op("s", lambda s, d=t1b[0][sl], a=m_pl[0][sl]: s.activation(
                d, a, AF.Identity, scale=2.0, bias=neg8r[:, :]))
            kb.op("s", lambda s, d=t1b[1][sl], a=m_pl[1][sl]: s.activation(
                d, a, AF.Identity, scale=2.0, bias=neg8r[:, :]), incs=[(sAct, 1)])

        def deq_lut(j):
            sl = j % 2
            rnd = j // 2
            kb.wait("v", sAct, j + 1)
            for p in range(2):
                kb.op("v", lambda v, p=p, sl=sl: v.scalar_tensor_tensor(
                    t2b[p][:, :], m_pl[p][sl][:, :], -2.0, t1b[p][sl][:, :], ALU.add, ALU.max))
            for p in range(2):
                kb.op("v", lambda v, p=p, sl=sl: v.scalar_tensor_tensor(
                    vab[p][:, :], m_pl[p][sl][:, :], 0.5, t2b[p][:, :], ALU.mult, ALU.max))
            for p in range(2):
                incs = [(sLut, 1)] if p == 1 else []
                kb.op("v", lambda v, p=p, sl=sl: v.tensor_tensor(
                    vvb[p][:, :], vab[p][:, :], s_pl[p][sl][:, :], ALU.mult), incs=incs)
            if rnd > 0:
                kb.wait("v", sSq[sl], 16 * rnd)
            scl_b = sccb2[sl][:, :].unsqueeze(2).broadcast_to([128, 64, 16])
            for p in range(2):
                incs = [(sDq[sl], 1)] if p == 1 else []
                kb.op("v", lambda v, p=p, sb_=scl_b, o=qout[sl]: v.tensor_tensor(
                    o[:, p::2].rearrange("q (bb j) -> q bb j", bb=64),
                    vvb[p][:, :].rearrange("q (bb j) -> q bb j", bb=64),
                    sb_, ALU.mult), incs=incs)
            kb.wait("g", sDq[sl], rnd + 1)
            dst = deq_jobs[j][2]
            qsrc = qout[sl].rearrange("p (t h) -> p t h", t=16)
            kb.dma("g", dst[:, 0:8, :], qsrc[:, 0:8, :], sSq[sl])
            kb.dma("g", dst[:, 8:16, :], qsrc[:, 8:16, :], sSq[sl])

        for i in range(len(deq_jobs)):
            deq_ints(i)
            deq_act(i)
            if i >= 1:
                deq_lut(i - 1)
        deq_lut(len(deq_jobs) - 1)
        kb.barrier()
        if stop_after == "p1":
            finish(kb, st, nc)
            return nc

        xqb = [mg(i * 2048, 2048) for i in range(2)]
        ktmp = [mg(4096 + i * 2048, 2048) for i in range(2)]
        ksw = mg(8192, 2048)
        rr1 = mg(10240, 2048)
        rr2 = mg(12288, 2048)
        ptbg = [mg(i * 2048, 2048) for i in range(3)]
        scfp = [mg(6144 + i * 2048, 2048, F32) for i in range(4)]
        scfg = [mg(6144 + i * 4096, 4096, F32) for i in range(2)]
        xbo = [mg(16384 + i * 4096, 4096, F32) for i in range(2)]
        obuf = [mg(24576 + i * 4096, 4096, F32) for i in range(2)]
        xbufL = [mg(i * 4096, 4096, F32) for i in range(2)]
        sqbL = [mg(8192 + i * 4096, 4096, F32) for i in range(2)]
        xnoL = [mg(16384 + i * 2048, 2048) for i in range(2)]
        stL = [mg(20480 + i * 64, 64, F32) for i in range(2)]
        wgb = [mg(i * 512, 512) for i in range(3)]
        wub = [mg(1536 + i * 512, 512) for i in range(3)]
        gsb = [mg(3072 + i * 512, 512) for i in range(4)]
        usb = [mg(5120 + i * 512, 512) for i in range(4)]
        tA = [mg(7168 + i * 512, 512) for i in range(4)]
        tBv = [mg(9216 + i * 512, 512) for i in range(4)]
        tC = [mg(11264 + i * 512, 512) for i in range(4)]
        intb = [mg(13312 + i * 512, 512) for i in range(4)]
        xnt = mg(15360, 8192)
        ilb = [mg(23552 + i * 512, 512) for i in range(2)]
        wdb = [mg(24576 + i * 2048, 2048) for i in range(2)]
        odb = [mg(28672 + i * 2048, 2048, F32) for i in range(2)]

        def zmm(ps_elem, sems):
            kb.op("t", lambda t, p=ps_elem: t.matmul(p, zrow[0:1, 0:1], zrow[0:1, 0:1],
                                                     start=False, stop=True), incs=sems)

        slab_uses = [0]

        def slab_refill(jobs):
            kb.wait("y", sSlabF, slab_uses[0])
            for d, s_ in jobs:
                kb.dma("y", d, s_, sSlab, transpose=True)
            kb.wait("t", sSlab)
            slab_uses[0] += 1

        xq_state = {"ptr": 0, "rnds": [0, 0]}

        def xq_load(src, width):
            i = xq_state["ptr"] % 2
            if xq_state["rnds"][i] > 0:
                kb.wait("y", sFxq[i], xq_state["rnds"][i])
            kb.dma("y", xqb[i][:, 0:width], src, sLxq[i], transpose=True)
            xq_state["rnds"][i] += 1
            xq_state["ptr"] += 1
            kb.wait("t", sLxq[i])
            return i

        for b in range(BPC):
            bSqA, bVs1, bSqr = sSqA.n, sVs1.n, sSqr.n
            bXnV = [s.n for s in sXnV]
            bXnS = [s.n for s in sXnS]
            bLxL = [s.n for s in sLxL]
            for k in range(8):
                for j in range(2):
                    if k > 0:
                        kb.wait("y", sXnV[j], bXnV[j] + k)
                    c = 2 * k + j
                    kb.dma("y", xbufL[j], T["x"][b, c * 128:(c + 1) * 128, :], sLxL[j])
                for j in range(2):
                    kb.wait("v", sLxL[j])
                    kb.op("v", lambda v, j=j: v.reduce_sum(stL[j][:, 0:1], xbufL[j], axis=AX.X))
                for j in range(2):
                    kb.wait("s", sLxL[j], bLxL[j] + 16 * (k + 1))
                    if k > 0:
                        kb.wait("s", sVs1, bVs1 + 2 * k)
                    kb.op("s", lambda s, j=j: s.activation(sqbL[j], xbufL[j], AF.Square,
                                                           accum_out=stL[j][:, 1:2]), incs=[(sSqA, 1)])
                kb.wait("v", sSqA, bSqA + 2 * (k + 1))
                if k > 0:
                    kb.wait("v", sSqr, bSqr + 2 * k)
                for j in range(2):
                    kb.op("v", lambda v, j=j: v.tensor_single_scalar(stL[j][:, 2:3], stL[j][:, 0:1], INV_D, ALU.mult))
                for j in range(2):
                    kb.op("v", lambda v, j=j: v.tensor_mul(stL[j][:, 3:4], stL[j][:, 2:3], stL[j][:, 2:3]))
                for j in range(2):
                    kb.op("v", lambda v, j=j: v.tensor_single_scalar(stL[j][:, 4:5], stL[j][:, 1:2], INV_D, ALU.mult))
                for j in range(2):
                    kb.op("v", lambda v, j=j: v.tensor_sub(stL[j][:, 5:6], stL[j][:, 4:5], stL[j][:, 3:4]),
                          incs=[(sVs1, 1)])
                kb.wait("s", sVs1, bVs1 + 2 * (k + 1))
                for j in range(2):
                    kb.op("s", lambda s, j=j: s.activation(stL[j][:, 6:7], stL[j][:, 5:6], AF.Sqrt,
                                                           bias=epsr[:, :]), incs=[(sSqr, 1)])
                kb.wait("v", sSqr, bSqr + 2 * (k + 1))
                for j in range(2):
                    kb.op("v", lambda v, j=j: v.reciprocal(stL[j][:, 7:8], stL[j][:, 6:7]))
                kb.drain("v")
                for j in range(2):
                    kb.op("v", lambda v, j=j: v.scalar_tensor_tensor(
                        stL[j][:, 8:9], stL[j][:, 2:3], -1.0, stL[j][:, 7:8], ALU.mult, ALU.mult))
                kb.drain("v")
                for j in range(2):
                    if k > 0:
                        kb.wait("v", sXnS[j], bXnS[j] + 16 * k)
                    kb.op("v", lambda v, j=j: v.tensor_scalar(
                        xnoL[j], xbufL[j], stL[j][:, 7:8], stL[j][:, 8:9], ALU.mult, ALU.add),
                        incs=[(sXnV[j], 1)])
                for j in range(2):
                    c = 2 * k + j
                    kb.wait("g", sXnV[j], bXnV[j] + k + 1)
                    kb.dma("g", T["xn_dq"][b * S + c * 128: b * S + (c + 1) * 128, :], xnoL[j], sXnS[j])
            kb.barrier()
            if stop_after == "ln":
                finish(kb, st, nc)
                return nc

            slab_refill([(slab[:, hc * 1024:(hc + 1) * 1024],
                          T["wkv_dq"][0:1024, hc * 128:(hc + 1) * 128]) for hc in range(HC)])
            if dbg and b == 0:
                kb.barrier()
                sDbgS = kb.sem()
                kb.dma("g", T["slab_dbg"], slab[:, :], sDbgS)
                kb.barrier()
            bMMk, bEvK, bRpK = sMMk.n, sEvK.n, sRpK.n
            for pr in range(4):
                if pr > 0:
                    kb.wait("t", sEvK, bEvK + pr)
                for hc in range(HC):
                    i = xq_load(T["xn_dq"][b * S:(b + 1) * S, hc * 128:(hc + 1) * 128], 2048)
                    if dbg and b == 0 and pr == 0 and hc == 0:
                        sDbgX = kb.sem()
                        kb.wait("g", sLxq[i])
                        kb.dma("g", T["xq_dbg"], xqb[i][:, :], sDbgX)
                        kb.wait("t", sDbgX)
                    for q in range(4):
                        incs = [(sFxq[i], 1)] if q == 3 else []
                        kb.op("t", lambda t, i=i, q=q, hc=hc, pr=pr: t.matmul(
                            PS_A[:, q * 512:(q + 1) * 512],
                            slab[:, hc * 1024 + pr * 128: hc * 1024 + (pr + 1) * 128],
                            xqb[i][:, q * 512:(q + 1) * 512],
                            start=(hc == 0), stop=False), incs=incs)
                for q in range(4):
                    incs = [(sMMk, 1)] if q == 3 else []
                    kb.op("t", lambda t, q=q, pr=pr: t.matmul(
                        PS_A[:, q * 512:(q + 1) * 512],
                        bkvr[0:1, pr * 128:(pr + 1) * 128],
                        onesr[0:1, q * 512:(q + 1) * 512],
                        start=False, stop=True), incs=incs)
                kb.wait("v", sMMk, bMMk + pr + 1)
                if pr > 1:
                    kb.wait("v", sRpK, bRpK + pr - 1)
                km = ktmp[pr % 2]
                kb.op("v", lambda v, km=km: v.tensor_copy(km, PS_A[:, :]), incs=[(sEvK, 1)])
                kb.drain("v")
                kb.op("v", lambda v, km=km: v.tensor_mul(rr1, km, ropeC[:, :]))
                kb.op("v", lambda v, km=km: v.stream_shuffle(ksw, km, _swapmask()))
                kb.drain("v")
                kb.op("v", lambda v: v.tensor_mul(rr2, ksw, ropeS[:, :]))
                kb.drain("v")
                kb.op("v", lambda v, pr=pr: v.tensor_add(
                    kt[0:64, (2 * pr) * 2048:(2 * pr + 1) * 2048], rr1[0:64, :], rr2[0:64, :]))
                kb.op("v", lambda v, pr=pr: v.tensor_add(
                    kt[64:128, (2 * pr + 1) * 2048:(2 * pr + 2) * 2048], rr1[64:128, :], rr2[64:128, :]),
                      incs=[(sRpK, 1)])
            bMMv, bEvV = sMMv.n, sEvV.n
            VPS = [PS_B[:, 0:512], PS_B[:, 512:1024], PS_C[:, 0:512], PS_C[:, 512:1024]]
            for tg in range(4):
                if tg > 0:
                    kb.wait("t", sEvV, bEvV + 4 * tg)
                for hc in range(HC):
                    i = xq_load(T["xn_dq"][b * S:(b + 1) * S, hc * 128:(hc + 1) * 128], 2048)
                    for q in range(4):
                        tc = tg * 4 + q
                        incs = [(sFxq[i], 1)] if q == 3 else []
                        kb.op("t", lambda t, i=i, q=q, tc=tc, hc=hc: t.matmul(
                            VPS[q], xqb[i][:, tc * 128:(tc + 1) * 128],
                            slab[:, hc * 1024 + 512: hc * 1024 + 1024],
                            start=(hc == 0), stop=False), incs=incs)
                        if tg == 3 and hc == 15 and q == 3:
                            zmm(PS_C[0:1, 512:513], [(sSlabF, 1)])
                for q in range(4):
                    incs = [(sMMv, 1)] if q == 3 else []
                    kb.op("t", lambda t, q=q: t.matmul(
                        VPS[q], onesr[0:1, 0:128], bkvr[0:1, 512:1024],
                        start=False, stop=True), incs=incs)
                kb.wait("v", sMMv, bMMv + tg + 1)
                for q in range(4):
                    tc = tg * 4 + q
                    dst = vsb[:, tc * 520:(tc + 1) * 520].rearrange("p (k c) -> p k c", k=8)[:, :, 0:64]
                    vsrc = VPS[q].rearrange("p (k c) -> p k c", k=8)
                    incs = [(sEvV, 4)] if q == 3 else []
                    kb.op("v", lambda v, dst=dst, vsrc=vsrc: v.tensor_copy(dst, vsrc), incs=incs)
            kb.barrier()
            if dbg and b == 0:
                sDbg = kb.sem()
                kb.dma("g", T["kt_dbg"], kt[:, 0:8192], sDbg)
                kb.dma("g", T["v_dbg"], vsb[:, :], sDbg)
                kb.barrier()
            if stop_after == "kv":
                finish(kb, st, nc)
                return nc

            for g in range(2):
                for qh in range(2):
                    qrow = b * S + qh * QHS
                    slab_refill([(slab[:, hc * 1024:(hc + 1) * 1024],
                                  T["wq_dq"][g * 1024:(g + 1) * 1024, hc * 128:(hc + 1) * 128])
                                 for hc in range(HC)])
                    QPS = [PS_A[:, 0:1024], PS_A[:, 1024:2048], PS_B[:, :], PS_C[:, :]]
                    bMMq, bEvQ, bRpQ = sMMq.n, sEvQ.n, sRpQ.n
                    for sw in range(2):
                        if sw > 0:
                            kb.wait("t", sEvQ, bEvQ + 4)
                        for hc in range(HC):
                            i = xq_load(T["xn_dq"][qrow:qrow + QHS, hc * 128:(hc + 1) * 128], 1024)
                            for p4 in range(4):
                                pg = sw * 4 + p4
                                stat = slab[:, hc * 1024 + pg * 128: hc * 1024 + (pg + 1) * 128]
                                for q in range(2):
                                    incs = []
                                    if p4 == 3 and q == 1:
                                        incs.append((sFxq[i], 1))
                                    kb.op("t", lambda t, i=i, q=q, hc=hc, stat=stat, p4=p4: t.matmul(
                                        QPS[p4][:, q * 512:(q + 1) * 512],
                                        stat,
                                        xqb[i][:, q * 512:(q + 1) * 512],
                                        start=(hc == 0), stop=False), incs=incs)
                                    if sw == 1 and hc == 15 and p4 == 3 and q == 1:
                                        zmm(PS_C[0:1, 0:1], [(sSlabF, 1)])
                        for p4 in range(4):
                            pg = sw * 4 + p4
                            bstat = bqr[0:1, g * 1024 + pg * 128: g * 1024 + (pg + 1) * 128]
                            for q in range(2):
                                incs = [(sMMq, 1)] if q == 1 else []
                                kb.op("t", lambda t, q=q, bstat=bstat, p4=p4: t.matmul(
                                    QPS[p4][:, q * 512:(q + 1) * 512],
                                    bstat,
                                    onesr[0:1, q * 512:(q + 1) * 512],
                                    start=False, stop=True), incs=incs)
                        for p4 in range(4):
                            pg = sw * 4 + p4
                            kb.wait("v", sMMq, bMMq + sw * 4 + p4 + 1)
                            if sw * 4 + p4 > 1:
                                kb.wait("v", sRpQ, bRpQ + sw * 4 + p4 - 1)
                            km = ktmp[p4 % 2]
                            kb.op("v", lambda v, km=km, p4=p4, QPS=QPS: v.tensor_copy(km[:, 0:1024], QPS[p4]),
                                  incs=[(sEvQ, 1)])
                            kb.drain("v")
                            cS = ropeC[:, qh * 1024:(qh + 1) * 1024]
                            sS = ropeS[:, qh * 1024:(qh + 1) * 1024]
                            kb.op("v", lambda v, km=km, cS=cS: v.tensor_mul(rr1[:, 0:1024], km[:, 0:1024], cS))
                            kb.op("v", lambda v, km=km: v.stream_shuffle(ksw[:, 0:1024], km[:, 0:1024], _swapmask()))
                            kb.drain("v")
                            kb.op("v", lambda v, sS=sS: v.tensor_mul(rr2[:, 0:1024], ksw[:, 0:1024], sS))
                            kb.drain("v")
                            kb.op("v", lambda v, pg=pg: v.tensor_add(
                                qt[:, pg * 1024:(pg + 1) * 1024], rr1[:, 0:1024], rr2[:, 0:1024]),
                                incs=[(sRpQ, 1)])
                    kb.wait("t", sRpQ, bRpQ + 8)
                    if dbg and b == 0 and g == 0 and qh == 0:
                        kb.barrier()
                        sDbg2 = kb.sem()
                        kb.dma("g", T["qt_dbg"], qt[:, :], sDbg2)
                        kb.barrier()
                    bEvA, bRcp, bRmm, bNrm = sEvA.n, sRcp.n, sRmm.n, sNrm.n
                    bDen = sDen.n
                    bSc, bScv, bExp, bPV = sSc.n, sScv.n, sExp.n, sPV.n
                    HEAD_ORDER = [0, 4, 1, 5, 2, 6, 3, 7, 8, 12, 9, 13, 10, 14, 11, 15]
                    hi_of = {hl_: i_ for i_, hl_ in enumerate(HEAD_ORDER)}
                    norm_at = {}
                    for j_ in range(8):
                        norm_at[max(hi_of[2 * j_], hi_of[2 * j_ + 1])] = j_
                    norm_pos = sorted(norm_at)
                    nidx = {p_: i_ for i_, p_ in enumerate(norm_pos)}
                    normle = {q_: len([p_ for p_ in norm_pos if p_ <= q_ - 2])
                              for q_ in range(16)}
                    release = {0: None, 1: None}
                    SPS4 = [PS_B[:, 0:512], PS_B[:, 512:1024],
                            PS_C[:, 0:512], PS_C[:, 512:1024]]

                    def hinfo(hi):
                        hl = HEAD_ORDER[hi]
                        kv = (g * 16 + hl) // 4
                        blk = (hl // 8) * 4 + (hl % 4)
                        return (hl, kv, (kv % 2) * 64, (kv // 2) * 2048,
                                ((hl % 8) // 4) * 64, blk * 1024)

                    def emit_score_mm(u, last):
                        hi_ = u // 32
                        kc = (u % 32) // 2
                        q = u % 2
                        _, kv, kprow, kpcol, qprow, qcol = hinfo(hi_)
                        sps = SPS4[u % 4]
                        incs = [(sSc, 1)]
                        kb.op("t", lambda t, sps=sps, q=q, kc=kc, kv=kv, qcol=qcol: t.matmul(
                                  sps,
                                  kt[0:128, kv * 2048 + kc * 128: kv * 2048 + (kc + 1) * 128],
                                  qt[0:128, qcol + q * 512: qcol + (q + 1) * 512],
                                  start=True, stop=True), incs=incs)

                    def emit_pv_mm(m, last):
                        hm = m // 32
                        kcm = (m % 32) // 2
                        qm = m % 2
                        _, kv, _, _, _, _ = hinfo(hm)
                        apsm = PS_A[:, (hm % 2) * 1024:((hm % 2) + 1) * 1024]
                        if kcm == 0 and qm == 0 and release[hm % 2] is not None:
                            rs, rc = release[hm % 2]
                            kb.wait("t", rs, rc)
                        ptbv = mega[:, ((m // 4) % 3) * 2048 + (m % 4) * 512:
                                    ((m // 4) % 3) * 2048 + (m % 4) * 512 + 512]
                        incs = [(sPV, 1)]
                        kb.op("t", lambda t, apsm=apsm, qm=qm, kcm=kcm, kv=kv, ptbv=ptbv: t.matmul(
                            apsm[0:65, qm * 512:(qm + 1) * 512],
                            vsb[:, kcm * 520 + kv * 65: kcm * 520 + (kv + 1) * 65],
                            ptbv,
                            start=(kcm == 0), stop=(kcm == 15)), incs=incs)

                    def emit_rmm(p):
                        j = norm_at[p]
                        sA = hi_of[2 * j] % 4
                        sB = hi_of[2 * j + 1] % 4
                        half = PS_A[:, (p % 2) * 1024:((p % 2) + 1) * 1024]
                        kb.wait("t", sRcp, bRcp + p + 1)
                        kb.wait("t", sEvA, bEvA + p + 1)
                        for qq in range(2):
                            kb.op("t", lambda t, half=half, sA=sA, qq=qq: t.matmul(
                                half[:, qq * 512:(qq + 1) * 512], oneslo[:, :],
                                rden[0:1, sA * 1024 + qq * 512: sA * 1024 + (qq + 1) * 512],
                                start=True, stop=False))
                        for qq in range(2):
                            incs = [(sRmm, 1)] if qq == 1 else []
                            kb.op("t", lambda t, half=half, sB=sB, qq=qq: t.matmul(
                                half[:, qq * 512:(qq + 1) * 512], oneshi[:, :],
                                rden[0:1, sB * 1024 + qq * 512: sB * 1024 + (qq + 1) * 512],
                                start=False, stop=True), incs=incs)

                    def emit_ttnorm(p):
                        j = norm_at[p]
                        half = PS_A[:, (p % 2) * 1024:((p % 2) + 1) * 1024]
                        kb.wait("v", sRmm, bRmm + nidx[p] + 1)
                        kb.op("v", lambda v, j=j, half=half: v.tensor_tensor(
                            at[:, j * 1024:(j + 1) * 1024], at[:, j * 1024:(j + 1) * 1024],
                            half, ALU.mult), incs=[(sNrm, 1)])
                        release[p % 2] = (sNrm, bNrm + nidx[p] + 1)

                    def emit_copy_pair(k):
                        kb.wait("v", sSc, bSc + 2 * k + 2)
                        if k >= 4:
                            kb.wait("v", sExp, bExp + k // 2 - 1)
                        srcp = PS_B if k % 2 == 0 else PS_C
                        kb.op("v", lambda v, k=k, srcp=srcp: v.tensor_copy(scfp[k % 4], srcp[:, :]),
                              incs=[(sScv, 1)])

                    def emit_exp_group(G):
                        kb.wait("s", sScv, bScv + 2 * G + 2)
                        if G >= 3:
                            kb.wait("s", sPV, bPV + 4 * G - 8)
                        kb.op("s", lambda s, G=G: s.activation(
                            ptbg[G % 3], scfg[G % 2], AF.Exp, scale=0.125), incs=[(sExp, 1)])

                    def emit_evac(hm):
                        hl = HEAD_ORDER[hm]
                        apsm = PS_A[:, (hm % 2) * 1024:((hm % 2) + 1) * 1024]
                        kb.wait("v", sPV, bPV + 32 * (hm + 1))
                        kb.op("v", lambda v, apsm=apsm, hl=hl: v.tensor_copy(
                            at[(hl % 2) * 64:(hl % 2) * 64 + 64, (hl // 2) * 1024:(hl // 2 + 1) * 1024],
                            apsm[0:64, :]), incs=[(sEvA, 1)])
                        release[hm % 2] = (sEvA, bEvA + hm + 1)

                    def emit_dencopy(hm):
                        apsm = PS_A[:, (hm % 2) * 1024:((hm % 2) + 1) * 1024]
                        kb.op("v", lambda v, apsm=apsm: v.tensor_copy(
                            rdenf[0:1, 0:1024], apsm[64:65, :]), incs=[(sDen, 1)])

                    def emit_ln(hm):
                        kb.wait("s", sDen, bDen + hm + 1)
                        kb.op("s", lambda s: s.activation(
                            lnb[0:1, 0:1024], rdenf[0:1, 0:1024], AF.Ln))

                    def emit_rexp(hm):
                        if hm >= 4 and normle[hm] > 0:
                            kb.wait("s", sRmm, bRmm + normle[hm])
                        kb.op("s", lambda s, hm=hm: s.activation(
                            rden[0:1, (hm % 4) * 1024:((hm % 4) + 1) * 1024],
                            lnb[0:1, 0:1024], AF.Exp, scale=-1.0), incs=[(sRcp, 1)])

                    def emit_pv_group(H):
                        kb.wait("t", sExp, bExp + H + 1)
                        for j in range(4):
                            emit_pv_mm(4 * H + j, j == 3)

                    for G in range(128):
                        u0 = 4 * G
                        hi, kcq = divmod(u0, 32)
                        if G >= 1:
                            kb.wait("t", sScv, bScv + 2 * G)
                        for j in range(4):
                            emit_score_mm(u0 + j, j == 3)
                        if G >= 3:
                            emit_pv_group(G - 3)
                        if kcq == 20 and hi - 1 in norm_at:
                            emit_rmm(hi - 1)
                        emit_copy_pair(2 * G)
                        emit_copy_pair(2 * G + 1)
                        if u0 >= 32:
                            hm = hi - 1
                            if kcq == 8:
                                emit_evac(hm)
                            elif kcq == 12:
                                emit_dencopy(hm)
                            elif kcq == 24 and hm in norm_at:
                                emit_ttnorm(hm)
                        emit_exp_group(G)
                        if u0 >= 32:
                            if kcq == 16:
                                emit_ln(hi - 1)
                            elif kcq == 20:
                                emit_rexp(hi - 1)
                    emit_pv_group(125)
                    emit_pv_group(126)
                    emit_pv_group(127)
                    emit_evac(15)
                    emit_dencopy(15)
                    emit_ln(15)
                    kb.drain("s")
                    emit_rexp(15)
                    emit_rmm(15)
                    emit_ttnorm(15)
                    if dbg and b == 0 and g == 0 and qh == 0:
                        kb.barrier()
                        sDbg3 = kb.sem()
                        kb.dma("g", T["at_dbg"], at[:, :], sDbg3)
                        kb.dma("g", T["rd_dbg"], rden[:, :], sDbg3)
                        kb.barrier()
                    slab_refill([(slab[:, jj * 2048:(jj + 1) * 2048],
                                  T["wo_dq"][0:2048, (g * 8 + jj) * 128:(g * 8 + jj + 1) * 128])
                                 for jj in range(8)])
                    kb.wait("t", sNrm, bNrm + 8)
                    kb.wait("t", sEvA, bEvA + 16)
                    kb.wait("t", sRcp, bRcp + 16)
                    bMMo, bEvO = sMMo.n, sEvO.n
                    bSo = [s.n for s in sSo]
                    bSo_g = [s.n for s in sSo]
                    for tcl in range(8):
                        OPS = ([PS_B[:, 0:512], PS_B[:, 512:1024], PS_C[:, 0:512], PS_C[:, 512:1024]]
                               if tcl % 2 == 0 else
                               [PS_A[:, 0:512], PS_A[:, 512:1024], PS_A[:, 1024:1536], PS_A[:, 1536:2048]])
                        if tcl >= 2:
                            kb.wait("t", sEvO, bEvO + (tcl - 1) * 4)
                        for jj in range(8):
                            for q in range(4):
                                incs = []
                                if g != 0 and jj == 7 and q == 3:
                                    incs.append((sMMo, 1))
                                kb.op("t", lambda t, OPS=OPS, jj=jj, q=q, tcl=tcl, g=g: t.matmul(
                                    OPS[q], at[:, jj * 1024 + tcl * 128: jj * 1024 + (tcl + 1) * 128],
                                    slab[:, jj * 2048 + q * 512: jj * 2048 + (q + 1) * 512],
                                    start=(jj == 0), stop=(g != 0 and jj == 7)), incs=incs)
                                if tcl == 7 and jj == 7 and q == 3:
                                    zmm(PS_A[0:1, 1536:1537], [(sSlabF, 1)])
                        if g == 0:
                            for q in range(4):
                                incs = [(sMMo, 1)] if q == 3 else []
                                kb.op("t", lambda t, OPS=OPS, q=q: t.matmul(
                                    OPS[q], onesr[0:1, 0:128], bor[0:1, q * 512:(q + 1) * 512],
                                    start=False, stop=True), incs=incs)
                        sl = tcl % 2
                        if g == 0:
                            if tcl == 0:
                                kb.wait("y", sNrm, bNrm + 8)
                            if tcl >= 2:
                                kb.wait("y", sSo[sl], bSo[sl] + 16 * (tcl // 2))
                            kb.dma("y", xbo[sl],
                                   T["x"][b, qh * QHS + tcl * 128: qh * QHS + (tcl + 1) * 128, :], sLxo[sl])
                            kb.wait("v", sLxo[sl])
                            kb.wait("v", sMMo, bMMo + tcl + 1)
                            if tcl >= 2:
                                kb.wait("v", sSo[sl], bSo[sl] + 16 * (tcl // 2))
                            for q in range(4):
                                incs = [(sEvO, 4)] if q == 3 else []
                                kb.op("v", lambda v, OPS=OPS, q=q, sl=sl: v.tensor_add(
                                    obuf[sl][:, q * 512:(q + 1) * 512],
                                    xbo[sl][:, q * 512:(q + 1) * 512], OPS[q]), incs=incs)
                            kb.wait("g", sEvO, bEvO + (tcl + 1) * 4)
                            kb.dma("g", T["out"][b, qh * QHS + tcl * 128: qh * QHS + (tcl + 1) * 128, :],
                                   obuf[sl], sSo[sl])
                        else:
                            kb.wait("v", sMMo, bMMo + tcl + 1)
                            if tcl >= 2:
                                kb.wait("v", sSo[sl], bSo[sl] + 16 * (tcl // 2))
                            for q in range(4):
                                incs = [(sEvO, 4)] if q == 3 else []
                                kb.op("v", lambda v, OPS=OPS, q=q, sl=sl: v.tensor_copy(
                                    obuf[sl][:, q * 512:(q + 1) * 512], OPS[q]), incs=incs)
                            kb.wait("g", sEvO, bEvO + (tcl + 1) * 4)
                            kb.dma("g", T["out"][b, qh * QHS + tcl * 128: qh * QHS + (tcl + 1) * 128, :],
                                   obuf[sl], sSo[sl], accum=ALU.add)
                    kb.barrier()
            if stop_after == "attn":
                finish(kb, st, nc)
                return nc

            wgbig = [mg(i * 2048, 2048) for i in range(2)]
            wubig = [mg(4096 + i * 2048, 2048) for i in range(2)]
            gs2 = [mg(8192 + i * 1024, 1024) for i in range(2)]
            us2 = [mg(10240 + i * 1024, 1024) for i in range(2)]
            tA2 = [mg(12288 + i * 1024, 1024) for i in range(2)]
            tB2 = [mg(14336 + i * 1024, 1024) for i in range(2)]
            tC2 = [mg(16384 + i * 1024, 1024) for i in range(2)]
            ib2 = [mg(18432 + i * 1024, 1024) for i in range(2)]
            il3 = ([mg(20480 + i * 1024, 1024) for i in range(2)]
                   + [mg(27648 + i * 1024, 1024) for i in range(2)])
            wd3 = ([mg(22528 + i * 512, 512) for i in range(2)]
                   + [mg(29696 + i * 512, 512) for i in range(2)])
            od3 = ([mg(23552 + i * 1024, 1024, F32) for i in range(4)]
                   + [mg(30720 + i * 1024, 1024, F32) for i in range(2)])
            xnt2 = slab[:, 0:16384]
            GPS2 = [PS_A[:, 0:1024], PS_B[:, :]]
            UPS2 = [PS_A[:, 1024:2048], PS_C[:, :]]
            DPS8 = [PS_A[:, i * 512:(i + 1) * 512] for i in range(4)] + \
                   [PS_B[:, i * 512:(i + 1) * 512] for i in range(2)] + \
                   [PS_C[:, i * 512:(i + 1) * 512] for i in range(2)]
            TBF = 1024
            for tb in range(2):
                gtb = b * 2 + tb
                par = gtb % 2
                if tb == 0:
                    for hc in range(HC):
                        kb.dma("y", xnt2[:, hc * 1024:(hc + 1) * 1024],
                               T["xn_dq"][b * S + tb * TBF: b * S + (tb + 1) * TBF,
                                          hc * 128:(hc + 1) * 128],
                               sXnt, transpose=True)
                kb.wait("t", sXnt)
                bGmm, bUmm, bGev, bUev, bGfree, bGel, bSi = (
                    sGmm.n, sUmm.n, sGev.n, sUev.n, sGfree.n, sGel.n, sSi.n)
                bTanh = sTanh.n
                nwg = [sFwg[i].n for i in range(2)]
                nwu = [sFwu[i].n for i in range(2)]
                for nq in range(128):
                    sl = nq % 2
                    if nwg[sl] > 0:
                        kb.wait("y", sFwg[sl], nwg[sl])
                    kb.dma("y", wgbig[sl], T["wg_dq"][nq], sLwg[sl], transpose=True)
                    nwg[sl] += 1
                    if nwu[sl] > 0:
                        kb.wait("y", sFwu[sl], nwu[sl])
                    kb.dma("y", wubig[sl], T["wu_dq"][nq], sLwu[sl], transpose=True)
                    nwu[sl] += 1
                    kb.wait("t", sLwg[sl])
                    if nq >= 2:
                        kb.wait("t", sGev, bGev + nq - 1)
                    for hc in range(HC):
                        for q in range(2):
                            incs = [(sFwg[sl], 1)] if (hc == 15 and q == 1) else []
                            kb.op("t", lambda t, sl=sl, hc=hc, q=q: t.matmul(
                                GPS2[sl][:, q * 512:(q + 1) * 512],
                                wgbig[sl][:, hc * 128:(hc + 1) * 128],
                                xnt2[:, hc * 1024 + q * 512: hc * 1024 + (q + 1) * 512],
                                start=(hc == 0), stop=(hc == 15)), incs=incs)
                    kb.wait("t", sLwu[sl])
                    if nq >= 2:
                        kb.wait("t", sUev, bUev + nq - 1)
                    for hc in range(HC):
                        for q in range(2):
                            incs = [(sFwu[sl], 1)] if (hc == 15 and q == 1) else []
                            kb.op("t", lambda t, sl=sl, hc=hc, q=q: t.matmul(
                                UPS2[sl][:, q * 512:(q + 1) * 512],
                                wubig[sl][:, hc * 128:(hc + 1) * 128],
                                xnt2[:, hc * 1024 + q * 512: hc * 1024 + (q + 1) * 512],
                                start=(hc == 0), stop=(hc == 15)), incs=incs)
                    kb.wait("v", sFwg[sl], nwg[sl])
                    if nq >= 2:
                        kb.wait("v", sGfree, bGfree + nq - 1)
                    kb.op("v", lambda v, sl=sl: v.tensor_copy(gs2[sl], GPS2[sl]), incs=[(sGev, 1)])
                    kb.wait("v", sFwu[sl], nwu[sl])
                    kb.op("v", lambda v, sl=sl: v.tensor_copy(us2[sl], UPS2[sl]), incs=[(sUev, 1)])
                    kb.op("v", lambda v, sl=sl: v.tensor_mul(tA2[sl], gs2[sl], gs2[sl]))
                    kb.drain("v")
                    kb.op("v", lambda v, sl=sl: v.tensor_scalar(
                        tB2[sl], tA2[sl], GC1 * GC0, GC0, ALU.mult, ALU.add))
                    kb.drain("v")
                    kb.op("v", lambda v, sl=sl: v.tensor_mul(tC2[sl], tB2[sl], gs2[sl]),
                          incs=[(sGel, 1)])
                    kb.wait("s", sGel, bGel + nq + 1)
                    kb.op("s", lambda s, sl=sl: s.activation(tA2[sl], tC2[sl], AF.Tanh),
                          incs=[(sTanh, 1)])
                    kb.wait("v", sTanh, bTanh + nq + 1)
                    kb.op("v", lambda v, sl=sl: v.tensor_scalar(tB2[sl], tA2[sl], 0.5, 0.5,
                                                               ALU.mult, ALU.add))
                    kb.op("v", lambda v, sl=sl: v.tensor_mul(tC2[sl], gs2[sl], us2[sl]))
                    kb.drain("v")
                    if nq >= 2:
                        kb.wait("v", sSi, bSi + 16 * (nq - 1))
                    kb.op("v", lambda v, sl=sl: v.tensor_mul(ib2[sl], tB2[sl], tC2[sl]),
                          incs=[(sGfree, 1)])
                    kb.wait("g", sGfree, bGfree + nq + 1)
                    kb.dma("g", T["int_dq"][par, nq * 128:(nq + 1) * 128, :], ib2[sl], sSi)
                if stop_after == "ffn1g" and b == 0 and tb == 0:
                    kb.barrier()
                    finish(kb, st, nc)
                    return nc
                kb.wait("y", sSi)
                kb.wait("s", sSi)
                kb.wait("t", sGev)
                kb.wait("t", sUev)
                bDmm, bDev = sDmm.n, sDev.n
                sSts6 = sSts + [sLf[0], sLf[1]]
                bSts = [s.n for s in sSts6]
                sLint4 = sLint + sLint2
                sFint4 = sFint + sFint2
                sLwd4 = sLwd + sLwd2
                nint = [s.n for s in sFint4]
                pending = []

                def mk_store(k, tc, r0, c0):
                    slot = tc % 6
                    qs = "g"

                    def emit():
                        kb.wait(qs, sDev, bDev + k + 1)
                        kb.dma(qs, T["out"][b, r0:r0 + 128, c0:c0 + 512],
                               od3[slot], sSts6[slot], accum=ALU.add)
                    return emit

                for oq in range(4):
                    oh, ohf = oq // 2, oq % 2
                    if oq > 0:
                        kb.wait("t", sDev, bDev + oq * 8)

                    lcnt = {}

                    def emit_load(ncc, oh=oh, ohf=ohf):
                        sl4 = ncc % 4
                        qn = "g" if ncc % 2 == 0 else "s"
                        if nint[sl4] > 0:
                            kb.wait(qn, sFint4[sl4], nint[sl4])
                            kb.wait("y", sFint4[sl4], nint[sl4])
                        kb.dma(qn, il3[sl4],
                               T["int_dq"][par, ncc * 128:(ncc + 1) * 128, 0:1024],
                               sLint4[sl4])
                        kb.dma("y", wd3[sl4],
                               T["wd_dq"][oh, ncc, ohf * 512:(ohf + 1) * 512, :],
                               sLwd4[sl4], transpose=True)
                        nint[sl4] += 1
                        lcnt[ncc] = 16 * nint[sl4]

                    for ncc in range(4):
                        emit_load(ncc)
                    for f in pending:
                        f()
                    pending = []
                    for ncc in range(128):
                        if ncc + 4 < 128:
                            emit_load(ncc + 4)
                        if oq == 3 and ncc % 8 == 4 and tb == 0:
                            hc_n = ncc // 8
                            kb.dma("y", xnt2[:, hc_n * 1024:(hc_n + 1) * 1024],
                                   T["xn_dq"][b * S + TBF: b * S + 2 * TBF,
                                              hc_n * 128:(hc_n + 1) * 128],
                                   sXnt, transpose=True)
                        sl4 = ncc % 4
                        kb.wait("t", sLint4[sl4], lcnt[ncc])
                        kb.wait("t", sLwd4[sl4], lcnt[ncc])
                        for tc in range(8):
                            incs = [(sFint4[sl4], 1)] if tc == 7 else []
                            kb.op("t", lambda t, sl4=sl4, tc=tc, ncc=ncc: t.matmul(
                                DPS8[tc],
                                il3[sl4][:, tc * 128:(tc + 1) * 128],
                                wd3[sl4][:, 0:512],
                                start=(ncc == 0), stop=(ncc == 127)), incs=incs)
                    zmm(DPS8[7][0:1, 0:1], [(sDmm, 1)])
                    for tc in range(8):
                        k = oq * 8 + tc
                        slot = tc % 6
                        kb.wait("v", sDmm, bDmm + oq + 1)
                        need = oq * (2 if slot < 2 else 1) + (1 if tc >= 6 else 0)
                        if need > 0:
                            kb.wait("v", sSts6[slot], bSts[slot] + 16 * need)
                        kb.op("v", lambda v, slot=slot, tc=tc: v.tensor_copy(
                            od3[slot], DPS8[tc]), incs=[(sDev, 1)])
                        pending.append(mk_store(k, tc, tb * TBF + tc * 128, oq * 512))
                for f in pending:
                    f()
                kb.barrier()
                if stop_after == "ffn1" and b == 0 and tb == 0:
                    finish(kb, st, nc)
                    return nc

        finish(kb, st, nc)
    return nc


def finish(kb, st, nc):
    kb.barrier()
    block = st.enter_context(nc.Block())

    @block.sync
    def _(eng):
        for f in kb.ops["y"]:
            f(eng)

    @block.vector
    def _(eng):
        for f in kb.ops["v"]:
            f(eng)

    @block.scalar
    def _(eng):
        for f in kb.ops["s"]:
            f(eng)

    @block.tensor
    def _(eng):
        for f in kb.ops["t"]:
            f(eng)

    @block.gpsimd
    def _(eng):
        for f in kb.ops["g"]:
            f(eng)


Q_HEAD_ORDER = [g * 16 + (pl // 4) * 8 + (pl % 4) + dd
                for g in range(2) for pl in range(8) for dd in (0, 4)]


def permute_q(wq, bq):
    wq_p = wq.reshape(32, 64, D)[Q_HEAD_ORDER].reshape(D, D)
    bq_p = bq.reshape(32, 64)[Q_HEAD_ORDER].reshape(1, D)
    return np.ascontiguousarray(wq_p), np.ascontiguousarray(bq_p)


def make_rope_tables():
    i = np.arange(0, DH, 2, dtype=np.float64) / DH
    freq = 1.0 / (10000.0 ** i)
    t = np.arange(S, dtype=np.float64)
    th = t[None, :] * freq[:, None]
    cos = np.cos(th)
    sin = np.sin(th)
    C64 = np.repeat(cos, 2, axis=0)
    S64 = np.repeat(sin, 2, axis=0)
    sign = np.where((np.arange(DH) % 2) == 0, -1.0, 1.0)[:, None]
    S64 = S64 * sign
    C = np.concatenate([C64, C64], axis=0)
    Sg = np.concatenate([S64, S64], axis=0)
    return (np.ascontiguousarray(C).astype(ml_dtypes.bfloat16),
            np.ascontiguousarray(Sg).astype(ml_dtypes.bfloat16))


def kernel(L0_attn_x, L0_attn_W_q, L0_attn_b_q, L0_attn_W_kv, L0_attn_b_kv,
           L0_attn_W_out, L0_attn_b_out, L0_ffn_gate_W_packed, L0_ffn_gate_scales,
           L0_ffn_up_W_packed, L0_ffn_up_scales, L0_ffn_down_W_packed,
           L0_ffn_down_scales):
    x = np.ascontiguousarray(np.asarray(L0_attn_x, dtype=np.float32))
    wq = np.ascontiguousarray(np.asarray(L0_attn_W_q, dtype=np.float32))
    bq = np.ascontiguousarray(np.asarray(L0_attn_b_q, dtype=np.float32)).reshape(1, D)
    wq, bq = permute_q(wq, bq)
    wkv = np.ascontiguousarray(np.asarray(L0_attn_W_kv, dtype=np.float32))
    bkv = np.ascontiguousarray(np.asarray(L0_attn_b_kv, dtype=np.float32)).reshape(1, 1024)
    wo = np.ascontiguousarray(np.asarray(L0_attn_W_out, dtype=np.float32))
    bo = np.ascontiguousarray(np.asarray(L0_attn_b_out, dtype=np.float32)).reshape(1, D)
    gw = np.ascontiguousarray(np.asarray(L0_ffn_gate_W_packed, dtype=np.int32))
    gs = np.ascontiguousarray(np.asarray(L0_ffn_gate_scales, dtype=np.float32))
    uw = np.ascontiguousarray(np.asarray(L0_ffn_up_W_packed, dtype=np.int32))
    us = np.ascontiguousarray(np.asarray(L0_ffn_up_scales, dtype=np.float32))
    dw = np.ascontiguousarray(np.asarray(L0_ffn_down_W_packed, dtype=np.int32))
    ds = np.ascontiguousarray(np.asarray(L0_ffn_down_scales, dtype=np.float32))
    ropec, ropes = make_rope_tables()

    os.environ.setdefault("GAUGE_NTFF_IGNORE_DMA", "1")

    nc = build_program()
    in_maps = []
    for c in range(NCORES):
        in_maps.append(dict(
            x=x[c * BPC:(c + 1) * BPC], wq=wq, bq=bq, wkv=wkv, bkv=bkv, wo=wo, bo=bo,
            gw=gw, gs=gs, uw=uw, us=us, dw=dw, ds=ds, ropeC=ropec, ropeS=ropes))
    res = run_bass_kernel_spmd(nc, in_maps, core_ids=list(range(NCORES)))
    out = np.concatenate([np.asarray(r["out"], dtype=np.float32) for r in res.results], axis=0)
    return out.reshape(16, S, D)
